# version 04
# baseline (speedup 1.0000x reference)
import contextlib
import os
import numpy as np
import ml_dtypes
import concourse.bass as bass
import concourse.mybir as mybir
from concourse.bass_utils import run_bass_kernel_spmd

F32 = mybir.dt.float32
BF16 = mybir.dt.bfloat16
AF = mybir.ActivationFunctionType
ALU = mybir.AluOpType
AX = mybir.AxisListType
BF = ml_dtypes.bfloat16

D = 4096
B = 2
T = 4096
TQ = 1024
NCH = 32
H = 32
HD = 128
HPC = 8
DIM = 512
DFF = 2048
DEPTH4 = 4
EPS = np.float32(1e-5)
EMSTRICT = int(os.environ.get("EMSTRICT", "2"))
TNOP = int(os.environ.get("TNOP", "128"))

ENGS = ("sync", "scalar", "vector", "tensor", "gpsimd")
CSEM = {"tensor": "cT", "vector": "cV", "scalar": "cS", "gpsimd": "cG"}
DSEM = {"sync": "dS", "gpsimd": "dG", "scalar": "dA"}
DSEMS = {"sync": "dSs", "gpsimd": "dGs", "scalar": "dAs"}
ALLSEMS = ["cT", "cV", "cS", "cG", "dS", "dG", "dA", "dSs", "dGs", "dAs", "cc"]


class Em:

    def __init__(self, strict=0):
        self.strict = strict
        self.steps = []
        self.counts = {s: 0 for s in ALLSEMS}
        self.resw = {}
        self.resr = {}
        self.know = {e: {s: 0 for s in ALLSEMS} for e in ENGS}

    def emit(self, engine, fn, reads=(), writes=(), dma=False, coll=False,
             small=False):
        if coll:
            sem, amt = "cc", 1
        elif dma:
            sem, amt = (DSEMS if small else DSEM)[engine], 16
        else:
            sem, amt = CSEM[engine], 1
        needed = {}
        if self.strict & 1:
            for s in ALLSEMS:
                if self.counts[s] > 0:
                    needed[s] = self.counts[s]

        def need(tok):
            if not tok:
                return
            s, v = tok
            if needed.get(s, 0) < v:
                needed[s] = v

        for r in reads:
            need(self.resw.get(r))
        for w in writes:
            need(self.resw.get(w))
            for t in self.resr.get(w, ()):
                need(t)
        waits = []
        own = CSEM.get(engine)
        is_async = dma or coll
        for s in ALLSEMS:
            v = needed.get(s, 0)
            if v <= 0:
                continue
            if (not is_async) and not (self.strict & 2) and s == own \
                    and v <= self.counts[own]:
                continue
            if self.know[engine][s] >= v:
                continue
            waits.append((s, v))
            self.know[engine][s] = v
        self.counts[sem] += amt
        tok = (sem, self.counts[sem])
        for w in writes:
            self.resw[w] = tok
            self.resr[w] = []
        for r in reads:
            self.resr.setdefault(r, []).append(tok)
        self.steps.append((engine, waits, fn, sem, amt))
        return tok

    def barrier(self):
        snap = dict(self.counts)
        for e in ENGS:
            waits = []
            for s in ALLSEMS:
                v = snap[s]
                if v <= 0:
                    continue
                if s == CSEM.get(e):
                    continue
                if self.know[e][s] >= v:
                    continue
                waits.append((s, v))
                self.know[e][s] = v
            if waits:
                self.steps.append((e, waits, None, None, 0))
        self.resw = {}
        self.resr = {}

    def finalize(self, block, sems):
        self.barrier()

        drain_mode = int(os.environ.get("EMDRAIN", "1"))

        def runner(name):
            def run(eng):
                for (e, waits, fn, sem, amt) in self.steps:
                    if e != name:
                        continue
                    for (s, v) in waits:
                        eng.wait_ge(sems[s], v)
                    if fn is not None:
                        inst = fn(eng)
                        if amt:
                            if drain_mode and amt == 1 and sem != "cc":
                                eng.drain()
                                if e == "tensor" and TNOP:
                                    eng.nop(cycle_cnt=TNOP).then_inc(
                                        sems[sem], amt)
                                else:
                                    eng.drain().then_inc(sems[sem], amt)
                            else:
                                inst.then_inc(sems[sem], amt)
            return run

        block.sync(runner("sync"))
        block.scalar(runner("scalar"))
        block.vector(runner("vector"))
        block.tensor(runner("tensor"))
        block.gpsimd(runner("gpsimd"))
        for s in ALLSEMS:
            assert self.counts[s] < 60000, f"semaphore {s} overflow: {self.counts[s]}"


def sb3(handle, nch, w):
    return handle[:].rearrange("p (c t) -> p c t", c=nch, t=w)



def _pc(vec, nch):
    return np.ascontiguousarray(np.asarray(vec, np.float32).reshape(nch, 128).T)


def _pc2(mat, nl, nch):
    m = np.asarray(mat, np.float32).reshape(nl, nch, 128)
    return np.ascontiguousarray(m.transpose(2, 0, 1).reshape(128, nl * nch))


def stage_consts():
    c = {}
    c["identb"] = np.eye(128, dtype=BF)
    c["ones1"] = np.ones((1, 128), dtype=BF)
    c["ones1f"] = np.ones((1, 128), np.float32)
    c["onescol"] = np.ones((128, 1), dtype=BF)
    c["onesf128"] = np.ones((128, 128), np.float32)
    c["onesb128"] = np.ones((128, 128), dtype=BF)
    c["zerof"] = np.zeros((128, 1), np.float32)
    q = np.arange(1024)[None, :]
    k = np.arange(128)[:, None]
    masks = [((q - k) >= 128 * m).astype(BF) for m in range(8)]
    c["masks"] = np.ascontiguousarray(np.concatenate(masks, axis=1))
    c["iota8"] = np.ascontiguousarray(
        np.tile(np.arange(1, 9, dtype=np.float32), (128, 8)))
    em8 = np.zeros((128, 128), dtype=BF)
    for e in range(8):
        em8[e, e * 16:(e + 1) * 16] = 1
    c["emask8"] = em8
    em4 = np.zeros((128, 32), dtype=BF)
    for e in range(4):
        em4[e, e * 8:(e + 1) * 8] = 1
    c["emask4"] = em4
    return c


def stage_core(inputs, core):
    b, sub = core // 4, core % 4
    hg = sub
    d = {}
    d.update(stage_consts())
    x = np.asarray(inputs["L0_attn_x"])
    d["xT"] = np.ascontiguousarray(x[b, sub * TQ:(sub + 1) * TQ, :].T)
    for nm, w in (("wqT", "L0_attn_Wq"), ("wkT", "L0_attn_Wk"), ("wvT", "L0_attn_Wv")):
        W = np.asarray(inputs[w])
        d[nm] = np.ascontiguousarray(W.T[:, hg * 1024:(hg + 1) * 1024])
    d["bq"] = _pc(np.asarray(inputs["L0_attn_bq"])[hg * 1024:(hg + 1) * 1024], 8)
    d["bk"] = _pc(np.asarray(inputs["L0_attn_bk"])[hg * 1024:(hg + 1) * 1024], 8)
    d["bvxf"] = np.ascontiguousarray(np.broadcast_to(
        np.asarray(inputs["L0_attn_bv"], np.float32)[None,
            hg * 1024:(hg + 1) * 1024], (128, 1024)))
    d["woT"] = np.ascontiguousarray(np.asarray(inputs["L0_attn_Wo"]).T)
    d["bo"] = _pc(inputs["L0_attn_bo"], 32)
    d["wbaseT"] = np.ascontiguousarray(np.asarray(inputs["L0_ffn_W_base"]).T)
    d["bbase"] = _pc(inputs["L0_ffn_b_base"], 32)
    d["wrT8"] = np.ascontiguousarray(np.asarray(inputs["L0_ffn_W_router"]).T)
    d["brow8"] = np.ascontiguousarray(
        np.broadcast_to(np.asarray(inputs["L0_ffn_b_router"], np.float32), (128, 8)))
    d["loraAT"] = np.ascontiguousarray(
        np.asarray(inputs["L0_ffn_lora_A"]).transpose(1, 0, 2).reshape(4096, 128))
    d["loraBf"] = np.ascontiguousarray(
        np.asarray(inputs["L0_ffn_lora_B"]).reshape(128, 4096))
    d["winT"] = np.ascontiguousarray(np.asarray(inputs["L1_attn_Win"]).T)
    d["binc"] = _pc(inputs["L1_attn_bin"], 4)
    for nm in ("ln1_w", "ln1_b", "ln2_w", "ln2_b", "sc_lin", "phz_re", "phz_im",
               "lc_re", "lc_im", "b2"):
        d[nm] = _pc2(inputs[f"L1_attn_{nm}"], 4, 4)
    d["b1c"] = _pc2(inputs["L1_attn_b1"], 4, 16)
    d["w1T"] = np.ascontiguousarray(
        np.asarray(inputs["L1_attn_W1"]).transpose(0, 2, 1).reshape(4 * 512, 2048))
    d["w2T"] = np.ascontiguousarray(
        np.asarray(inputs["L1_attn_W2"]).transpose(0, 2, 1).reshape(4 * 2048, 512))
    d["lnlw"] = _pc(inputs["L1_attn_lnl_w"], 4)
    d["lnlb"] = _pc(inputs["L1_attn_lnl_b"], 4)
    d["woutT"] = np.ascontiguousarray(np.asarray(inputs["L1_attn_Wout"]).T)
    d["bout"] = _pc(inputs["L1_attn_bout"], 32)
    tb = np.zeros((128, 4), np.float32)
    tb[:, 0] = (sub >> 0) & 1
    tb[:, 1] = (sub >> 1) & 1
    tb[:, 2] = 1.0 - tb[:, 0]
    tb[:, 3] = 1.0 - tb[:, 1]
    d["t0bits"] = tb
    cm = np.zeros((128, 4), np.float32)
    cm[:, sub] = 1.0
    d["cmask"] = cm
    d["wffT"] = np.ascontiguousarray(np.asarray(inputs["L1_ffn_W"]).T)
    d["bff"] = _pc(inputs["L1_ffn_b"], 32)
    d["wrT4"] = np.ascontiguousarray(np.asarray(inputs["L1_ffn_Wr"]).T)
    d["AT4"] = np.ascontiguousarray(
        np.asarray(inputs["L1_ffn_A"]).transpose(2, 0, 1).reshape(4096, 32))
    bmt = np.zeros((128, 4096), np.float32)
    bmt[0:32] = np.asarray(inputs["L1_ffn_Bm"]).transpose(0, 2, 1).reshape(32, 4096)
    d["BmT4"] = bmt
    return d



def build_launch1():
    nc = bass.Bass("TRN2", target_bir_lowering=False, debug=False)
    dxT = nc.dram_tensor("xT", [D, TQ], F32, kind="ExternalInput").ap()
    donesf = nc.dram_tensor("d_onesf", [128, 128], F32, kind="ExternalInput").ap()
    donesb = nc.dram_tensor("d_onesb", [128, 128], BF16, kind="ExternalInput").ap()
    dout = nc.dram_tensor("xlnq", [D, TQ], BF16, kind="ExternalOutput").ap()
    dxbe = nc.dram_tensor("xbecho", [D, TQ], BF16,
                          kind="ExternalOutput" if os.environ.get("DBGL1")
                          else "Internal").ap()
    dste = nc.dram_tensor("stecho", [1, 4 * TQ], F32,
                          kind="ExternalOutput" if os.environ.get("DBGL1")
                          else "Internal").ap()

    em = Em(strict=EMSTRICT)
    with contextlib.ExitStack() as stack:
        E = stack.enter_context
        xb = E(nc.sbuf_tensor("s_xb", [128, NCH * TQ], BF16))
        xsq = E(nc.sbuf_tensor("s_xsq", [128, NCH * TQ], BF16))
        onesb_t = E(nc.sbuf_tensor("s_onesb", [128, 128], BF16))
        onesf_t = E(nc.sbuf_tensor("s_onesf", [128, 128], F32))
        onescol = onesb_t[:, 0:1]
        ones1f = onesf_t[0:1, :]
        stats = E(nc.sbuf_tensor("s_stats", [1, 4 * TQ], F32))
        mrs = E(nc.sbuf_tensor("s_mrs", [128, 2 * TQ], F32))
        psum_s = E(nc.psum_tensor("p_ps", [1, 2 * TQ], F32))
        psum_b = E(nc.psum_tensor("p_pb", [128, 2 * TQ], F32))
        sems = {}
        for s in ALLSEMS:
            sems[s] = E(nc.semaphore(s))
        block = E(nc.Block())

        em.emit("gpsimd", lambda g: g.dma_start(out=onesb_t[:], in_=donesb),
                writes=["onescol"], dma=True)
        em.emit("gpsimd", lambda g: g.dma_start(out=onesf_t[:], in_=donesf),
                writes=["ones1f"], dma=True)
        xb3 = sb3(xb, NCH, TQ)
        for ic in range(NCH):
            em.emit("gpsimd",
                    (lambda ic: lambda g: g.dma_start(
                        out=xb3[:, ic], in_=dxT[ic * 128:(ic + 1) * 128, :]))(ic),
                    writes=[f"xb{ic}"], dma=True)
        if os.environ.get("DBGL1"):
            for ic in range(NCH):
                em.emit("sync",
                        (lambda ic: lambda s: s.dma_start(
                            out=dxbe[ic * 128:(ic + 1) * 128, :],
                            in_=xb3[:, ic]))(ic),
                        reads=[f"xb{ic}"], writes=["xbe"], dma=True)
        for ic in range(NCH):
            sq = ic % 2
            em.emit("vector",
                    (lambda ic, sq: lambda v: v.tensor_mul(
                        out=xsq[:, sq * TQ:(sq + 1) * TQ],
                        in0=xb3[:, ic], in1=xb3[:, ic]))(ic, sq),
                    reads=[f"xb{ic}"], writes=[f"xsq{sq}"])
            for hh in range(2):
                em.emit("tensor",
                        (lambda ic, hh: lambda t: t.matmul(
                            psum_s[0:1, hh * 512:(hh + 1) * 512], onescol[:],
                            xb3[:, ic, hh * 512:(hh + 1) * 512],
                            start=(ic == 0), stop=(ic == NCH - 1)))(ic, hh),
                        reads=[f"xb{ic}", "onescol"], writes=[f"psx{hh}"])
                em.emit("tensor",
                        (lambda ic, sq, hh: lambda t: t.matmul(
                            psum_s[0:1, TQ + hh * 512:TQ + (hh + 1) * 512],
                            onescol,
                            xsq[:, sq * TQ + hh * 512:sq * TQ + (hh + 1) * 512],
                            start=(ic == 0), stop=(ic == NCH - 1)))(ic, sq, hh),
                        reads=[f"xsq{sq}", "onescol"], writes=[f"psq{hh}"])
        em.emit("vector", lambda v: v.tensor_scalar_mul(
            out=stats[0:1, 0:TQ], in0=psum_s[0:1, 0:TQ],
            scalar1=float(1.0 / D)),
            reads=["psx0", "psx1"], writes=["st_mean"])
        em.emit("vector", lambda v: v.tensor_scalar_mul(
            out=stats[0:1, TQ:2 * TQ], in0=psum_s[0:1, TQ:2 * TQ],
            scalar1=float(1.0 / D)),
            reads=["psq0", "psq1"], writes=["st_msq"])
        em.emit("vector", lambda v: v.tensor_mul(
            out=stats[0:1, 2 * TQ:3 * TQ], in0=stats[0:1, 0:TQ],
            in1=stats[0:1, 0:TQ]),
            reads=["st_mean"], writes=["st_m2"])
        em.emit("vector", lambda v: v.tensor_sub(
            out=stats[0:1, TQ:2 * TQ], in0=stats[0:1, TQ:2 * TQ],
            in1=stats[0:1, 2 * TQ:3 * TQ]),
            reads=["st_msq", "st_m2"], writes=["st_var"])
        em.emit("scalar", lambda s: s.activation(
            out=stats[0:1, TQ:2 * TQ], in_=stats[0:1, TQ:2 * TQ],
            func=AF.Ln, bias=EPS),
            reads=["st_var"], writes=["st_lnv"])
        em.emit("scalar", lambda s: s.activation(
            out=stats[0:1, TQ:2 * TQ], in_=stats[0:1, TQ:2 * TQ],
            func=AF.Exp, scale=np.float32(-0.5)),
            reads=["st_lnv"], writes=["st_rstd"])
        for hh in range(2):
            em.emit("tensor",
                    (lambda hh: lambda t: t.matmul(
                        psum_b[:, hh * 512:(hh + 1) * 512], ones1f,
                        stats[0:1, hh * 512:(hh + 1) * 512],
                        start=True, stop=True))(hh),
                    reads=["st_mean", "ones1f"], writes=["pb_mu"])
            em.emit("tensor",
                    (lambda hh: lambda t: t.matmul(
                        psum_b[:, TQ + hh * 512:TQ + (hh + 1) * 512], ones1f,
                        stats[0:1, TQ + hh * 512:TQ + (hh + 1) * 512],
                        start=True, stop=True))(hh),
                    reads=["st_rstd", "ones1f"], writes=["pb_rstd"])
        em.emit("scalar", lambda s: s.copy(out=mrs[:, 0:TQ], in_=psum_b[:, 0:TQ]),
                reads=["pb_mu"], writes=["mu_b"])
        em.emit("scalar", lambda s: s.copy(out=mrs[:, TQ:2 * TQ],
                                           in_=psum_b[:, TQ:2 * TQ]),
                reads=["pb_rstd"], writes=["rstd_b"])
        if os.environ.get("DBGL1"):
            em.emit("sync", lambda s: s.dma_start(out=dste, in_=stats[:]),
                    reads=["st_mean", "st_rstd", "st_m2"], writes=["ste"],
                    dma=True)
        for ic in range(NCH):
            em.emit("vector",
                    (lambda ic: lambda v: v.tensor_sub(
                        out=xb3[:, ic], in0=xb3[:, ic], in1=mrs[:, 0:TQ]))(ic),
                    reads=[f"xb{ic}", "mu_b"], writes=[f"xb{ic}"])
            em.emit("vector",
                    (lambda ic: lambda v: v.tensor_mul(
                        out=xb3[:, ic], in0=xb3[:, ic], in1=mrs[:, TQ:2 * TQ]))(ic),
                    reads=[f"xb{ic}", "rstd_b"], writes=[f"xb{ic}"])
            em.emit("sync",
                    (lambda ic: lambda s: s.dma_start(
                        out=dout[ic * 128:(ic + 1) * 128, :], in_=xb3[:, ic]))(ic),
                    reads=[f"xb{ic}"], writes=[f"out{ic}"], dma=True)
        em.finalize(block, sems)
    return nc



RSQHD = np.float32(1.0 / np.sqrt(128.0))


def build_launch2():
    nc = bass.Bass("TRN2", target_bir_lowering=False, debug=False)
    dxln = nc.dram_tensor("xlnf", [D, T], BF16, kind="ExternalInput").ap()
    dwT = {p: nc.dram_tensor(f"w{p}T", [D, 1024], F32, kind="ExternalInput").ap()
           for p in "qkv"}
    dbq = nc.dram_tensor("bq", [128, 8], F32, kind="ExternalInput").ap()
    dbk = nc.dram_tensor("bk", [128, 8], F32, kind="ExternalInput").ap()
    dbvxf = nc.dram_tensor("bvxf", [128, 1024], F32, kind="ExternalInput").ap()
    dmasks = nc.dram_tensor("masks", [128, 8 * 1024], BF16, kind="ExternalInput").ap()
    donesf = nc.dram_tensor("d_onesf", [128, 128], F32, kind="ExternalInput").ap()
    donesb = nc.dram_tensor("d_onesb", [128, 128], BF16, kind="ExternalInput").ap()
    dav = nc.dram_tensor("av", [1024, T], BF16, kind="ExternalOutput").ap()
    qkvkind = "ExternalOutput" if os.environ.get("DBGL2") else "Internal"
    ddbg = nc.dram_tensor("dbgsums", [32, 1024], F32, kind=qkvkind).ap()
    ddbg2 = nc.dram_tensor("dbgnegc", [8, 24], F32, kind=qkvkind).ap()
    dq = nc.dram_tensor("dq", [1024, T], BF16, kind=qkvkind).ap()
    dk = nc.dram_tensor("dk", [1024, T], BF16, kind=qkvkind).ap()
    dv = nc.dram_tensor("dv", [T, 1024], BF16, kind=qkvkind).ap()

    em = Em(strict=EMSTRICT)
    with contextlib.ExitStack() as stack:
        E = stack.enter_context
        sems = {s: E(nc.semaphore(s)) for s in ALLSEMS}
        block = E(nc.Block())
        onesb_t = E(nc.sbuf_tensor("s_onesb", [128, 128], BF16))
        onesf_t = E(nc.sbuf_tensor("s_onesf", [128, 128], F32))
        onescol = onesb_t[:, 0:1]
        ones1 = onesb_t[0:1, :]
        ones1f = onesf_t[0:1, :]
        em.emit("gpsimd", lambda g: g.dma_start(out=onesb_t[:], in_=donesb),
                writes=["onescol", "ones1"], dma=True)
        em.emit("gpsimd", lambda g: g.dma_start(out=onesf_t[:], in_=donesf),
                writes=["ones1f"], dma=True)

        with contextlib.ExitStack() as s2:
            E2 = s2.enter_context
            wt = E2(nc.sbuf_tensor("s_wt", [128, NCH * 1024], BF16))
            rhs = E2(nc.sbuf_tensor("s_rhs", [128, 2 * NCH * 512], BF16))
            bqk = E2(nc.sbuf_tensor("s_bqk", [128, 16], F32))
            bvx = E2(nc.sbuf_tensor("s_bvx", [128, 1024], F32))
            qkst = E2(nc.sbuf_tensor("s_qkst", [128, 4 * 512], BF16))
            vst = E2(nc.sbuf_tensor("s_vst", [128, 2 * 1024], BF16))
            pqk = [E2(nc.psum_tensor(f"p_pqk{i}", [128, 512], F32)) for i in range(2)]
            pv = [E2(nc.psum_tensor(f"p_pv{i}", [128, 1024], F32)) for i in range(2)]
            pbc = E2(nc.psum_tensor("p_pbc", [128, 1024], F32))
            wt3 = sb3(wt, NCH, 1024)
            rhs3 = sb3(rhs, 2 * NCH, 512)
            qkst3 = sb3(qkst, 4, 512)
            vst3 = sb3(vst, 2, 1024)

            em.emit("sync", lambda s: s.dma_start(out=bqk[:, 0:8], in_=dbq),
                    writes=["bq"], dma=True)
            em.emit("sync", lambda s: s.dma_start(out=bqk[:, 8:16], in_=dbk),
                    writes=["bk"], dma=True)
            em.emit("sync", lambda s: s.dma_start(out=bvx[:], in_=dbvxf),
                    writes=["bvx"], dma=True)

            for pi, proj in enumerate("qkv"):
                dst = {"q": dq, "k": dk, "v": dv}[proj]
                for ic in range(NCH):
                    em.emit("gpsimd",
                            (lambda pi, proj, ic: lambda g: g.dma_start(
                                out=wt3[:, ic],
                                in_=dwT[proj][ic * 128:(ic + 1) * 128, :]))(pi, proj, ic),
                            writes=[f"wt{ic}"], dma=True)
                for ts_ in range(8):
                    par = ts_ % 2
                    em.emit("sync",
                            (lambda par, ts_: lambda s: s.dma_start(
                                out=rhs3[:, par * NCH:(par + 1) * NCH],
                                in_=dxln[:, ts_ * 512:(ts_ + 1) * 512].rearrange(
                                    "(c p) t -> p c t", p=128)))(par, ts_),
                            reads=[], writes=[f"rhs{par}"], dma=True)
                    if proj in "qk":
                        for oc in range(8):
                            pp = oc % 2

                            def mmg(pp, par, oc):
                                def f(t):
                                    for ic in range(NCH):
                                        inst = t.matmul(
                                            pqk[pp][:],
                                            wt3[:, ic, oc * 128:(oc + 1) * 128],
                                            rhs3[:, par * NCH + ic],
                                            start=(ic == 0), stop=(ic == NCH - 1))
                                    return inst
                                return f
                            em.emit("tensor", mmg(pp, par, oc),
                                    reads=[f"rhs{par}"] + [f"wt{ic}" for ic in range(NCH)],
                                    writes=[f"pqk{pp}"])
                            st = oc % 4
                            em.emit("scalar",
                                    (lambda pp, st, oc, pi: lambda s: s.activation(
                                        out=qkst3[:, st], in_=pqk[pp][:],
                                        func=AF.Identity,
                                        bias=bqk[:, pi * 8 + oc:pi * 8 + oc + 1]))(
                                            pp, st, oc, pi),
                                    reads=[f"pqk{pp}", "bq", "bk"],
                                    writes=[f"qkst{st}"])
                            em.emit("sync",
                                    (lambda st, oc, ts_, dst: lambda s: s.dma_start(
                                        out=dst[oc * 128:(oc + 1) * 128,
                                                ts_ * 512:(ts_ + 1) * 512],
                                        in_=qkst3[:, st]))(st, oc, ts_, dst),
                                    reads=[f"qkst{st}"], writes=[f"dqk{st}"], dma=True)
                    else:
                        for tc in range(4):
                            pp = tc % 2

                            def mmv(pp, par, tc):
                                def f(t):
                                    for hh in range(2):
                                        for ic in range(NCH):
                                            inst = t.matmul(
                                                pv[pp][:, hh * 512:(hh + 1) * 512],
                                                rhs3[:, par * NCH + ic,
                                                     tc * 128:(tc + 1) * 128],
                                                wt3[:, ic, hh * 512:(hh + 1) * 512],
                                                start=(ic == 0),
                                                stop=(ic == NCH - 1))
                                    return inst
                                return f
                            em.emit("tensor", mmv(pp, par, tc),
                                    reads=[f"rhs{par}"] + [f"wt{ic}" for ic in range(NCH)],
                                    writes=[f"pv{pp}"])
                            em.emit("vector",
                                    (lambda pp, tc: lambda v: v.tensor_add(
                                        out=vst3[:, pp], in0=pv[pp][:],
                                        in1=bvx[:]))(pp, tc),
                                    reads=[f"pv{pp}", "bvx"], writes=[f"vst{pp}"])
                            em.emit("sync",
                                    (lambda pp, ts_, tc: lambda s: s.dma_start(
                                        out=dv[ts_ * 512 + tc * 128:
                                               ts_ * 512 + (tc + 1) * 128, :],
                                        in_=vst3[:, pp]))(pp, ts_, tc),
                                    reads=[f"vst{pp}"], writes=[f"dv{pp}"], dma=True)
        em.barrier()

        with contextlib.ExitStack() as s3:
            E3 = s3.enter_context
            ksb = E3(nc.sbuf_tensor("s_ksb", [128, T], BF16))
            qsb = E3(nc.sbuf_tensor("s_qsb", [128, T], BF16))
            vsb = E3(nc.sbuf_tensor("s_vsb", [128, NCH * 128], BF16))
            masks = E3(nc.sbuf_tensor("s_masks", [128, 8 * 1024], BF16))
            sqs = E3(nc.sbuf_tensor("s_sqs", [128, 512], BF16))
            nmax = E3(nc.sbuf_tensor("s_nmax", [1, 24], F32))
            negc = E3(nc.sbuf_tensor("s_negc", [128, 1], F32))
            ncb = E3(nc.sbuf_tensor("s_ncb", [1, 1], BF16))
            pt = E3(nc.sbuf_tensor("s_pt", [128, 2 * 1024], BF16))
            rsum = E3(nc.sbuf_tensor("s_rsum", [1, 1024], F32))
            rbb = E3(nc.sbuf_tensor("s_rbb", [128, 1024], F32))
            avn = E3(nc.sbuf_tensor("s_avn", [128, 1024], BF16))
            pst = [E3(nc.psum_tensor(f"p_pst{i}", [128, 1024], F32)) for i in range(2)]
            pav = E3(nc.psum_tensor("p_pav", [128, 1024], F32))
            pmx = E3(nc.psum_tensor("p_pmx", [128, 1024], F32))
            vsb3 = sb3(vsb, NCH, 128)
            masks3 = sb3(masks, 8, 1024)
            pt3 = sb3(pt, 2, 1024)

            em.emit("sync", lambda s: s.dma_start(out=masks[:], in_=dmasks),
                    writes=["masks"], dma=True)
            for h in range(HPC):
                em.emit("sync",
                        (lambda h: lambda s: s.dma_start(
                            out=ksb[:], in_=dk[h * 128:(h + 1) * 128, :]))(h),
                        writes=["ksb"], dma=True)
                em.emit("sync",
                        (lambda h: lambda s: s.dma_start(
                            out=qsb[:], in_=dq[h * 128:(h + 1) * 128, :]))(h),
                        writes=["qsb"], dma=True)
                em.emit("gpsimd",
                        (lambda h: lambda g: g.dma_start(
                            out=vsb3[:], in_=dv[:, h * 128:(h + 1) * 128].rearrange(
                                "(c p) n -> p c n", p=128)))(h),
                        writes=["vsb"], dma=True)
                for gi, src in ((0, qsb), (1, ksb)):
                    for piece in range(8):
                        em.emit("vector",
                                (lambda src, piece: lambda v: v.tensor_mul(
                                    out=sqs[:],
                                    in0=src[:, piece * 512:(piece + 1) * 512],
                                    in1=src[:, piece * 512:(piece + 1) * 512]))(
                                        src, piece),
                                reads=["ksb" if gi else "qsb"], writes=["sqs"])
                        em.emit("tensor", lambda t: t.matmul(
                            pmx[0:1, 0:512], onescol, sqs[:],
                            start=True, stop=True),
                            reads=["sqs", "onescol"], writes=["pmx"])
                        em.emit("vector",
                                (lambda gi, piece: lambda v: v.reduce_max(
                                    out=nmax[0:1, gi * 8 + piece:gi * 8 + piece + 1],
                                    in_=pmx[0:1, 0:512], axis=AX.X))(gi, piece),
                                reads=["pmx"], writes=[f"nm{gi}_{piece}"])
                em.emit("vector", lambda v: v.reduce_max(
                    out=nmax[0:1, 16:17], in_=nmax[0:1, 0:8], axis=AX.X),
                    reads=[f"nm0_{p}" for p in range(8)], writes=["nmq"])
                em.emit("vector", lambda v: v.reduce_max(
                    out=nmax[0:1, 17:18], in_=nmax[0:1, 8:16], axis=AX.X),
                    reads=[f"nm1_{p}" for p in range(8)], writes=["nmk"])
                em.emit("vector", lambda v: v.tensor_mul(
                    out=nmax[0:1, 18:19], in0=nmax[0:1, 16:17],
                    in1=nmax[0:1, 17:18]),
                    reads=["nmq", "nmk"], writes=["nmp"])
                em.emit("scalar", lambda s: s.activation(
                    out=nmax[0:1, 19:20], in_=nmax[0:1, 18:19], func=AF.Sqrt),
                    reads=["nmp"], writes=["nms"])
                em.emit("vector", lambda v: v.tensor_scalar_mul(
                    out=nmax[0:1, 20:21], in0=nmax[0:1, 19:20],
                    scalar1=float(-1.05 / np.sqrt(128.0))),
                    reads=["nms"], writes=["nmc"])
                em.emit("vector", lambda v: v.tensor_copy(
                    out=ncb[:], in_=nmax[0:1, 20:21]),
                    reads=["nmc"], writes=["ncb"])
                em.emit("tensor", lambda t: t.matmul(
                    pmx[:, 0:1], ones1, ncb[:], start=True, stop=True),
                    reads=["ncb", "ones1"], writes=["pmx"])
                em.emit("scalar", lambda s: s.copy(out=negc[:], in_=pmx[:, 0:1]),
                        reads=["pmx"], writes=["negc"])
                if os.environ.get("DBGL2"):
                    em.emit("sync",
                            (lambda h: lambda s: s.dma_start(
                                out=ddbg2[h:h + 1, :], in_=nmax[0:1, 0:24]))(h),
                            reads=["nms", "nmc"], writes=["dbg2"], dma=True,
                            small=True)
                for s_ in range(4):
                    njc = 8 * s_ + 8
                    for j in range(njc):
                        pp = j % 2
                        ptp = j % 2

                        def mmst(pp, j, s_):
                            def f(t):
                                for hh in range(2):
                                    inst = t.matmul(
                                        pst[pp][:, hh * 512:(hh + 1) * 512],
                                        ksb[:, j * 128:(j + 1) * 128],
                                        qsb[:, s_ * 1024 + hh * 512:
                                            s_ * 1024 + (hh + 1) * 512],
                                        start=True, stop=True)
                                return inst
                            return f
                        em.emit("tensor", mmst(pp, j, s_),
                                reads=["ksb", "qsb"], writes=[f"pst{pp}"])
                        em.emit("scalar",
                                (lambda pp, ptp: lambda s: s.activation(
                                    out=pt3[:, ptp], in_=pst[pp][:], func=AF.Exp,
                                    bias=negc[:], scale=RSQHD))(pp, ptp),
                                reads=[f"pst{pp}", "negc"], writes=[f"pt{ptp}"])
                        if j >= 8 * s_:
                            m = j - 8 * s_
                            em.emit("vector",
                                    (lambda ptp, m: lambda v: v.tensor_mul(
                                        out=pt3[:, ptp], in0=pt3[:, ptp],
                                        in1=masks3[:, m]))(ptp, m),
                                    reads=[f"pt{ptp}", "masks"],
                                    writes=[f"pt{ptp}"])

                        def mmsum(ptp, j, njc):
                            def f(t):
                                for hh in range(2):
                                    inst = t.matmul(
                                        pmx[0:1, hh * 512:(hh + 1) * 512],
                                        onescol,
                                        pt3[:, ptp, hh * 512:(hh + 1) * 512],
                                        start=(j == 0), stop=(j == njc - 1))
                                return inst
                            return f
                        em.emit("tensor", mmsum(ptp, j, njc),
                                reads=[f"pt{ptp}", "onescol"], writes=["pmx"])

                        def mmav(ptp, j, njc):
                            def f(t):
                                for hh in range(2):
                                    inst = t.matmul(
                                        pav[:, hh * 512:(hh + 1) * 512],
                                        vsb3[:, j],
                                        pt3[:, ptp, hh * 512:(hh + 1) * 512],
                                        start=(j == 0), stop=(j == njc - 1))
                                return inst
                            return f
                        em.emit("tensor", mmav(ptp, j, njc),
                                reads=[f"pt{ptp}", "vsb"], writes=["pav"])
                    if os.environ.get("DBGL2"):
                        em.emit("vector", lambda v: v.tensor_copy(
                            out=rsum[:], in_=pmx[0:1, 0:1024]),
                            reads=["pmx"], writes=["rsum"])
                        em.emit("sync",
                                (lambda h, s_: lambda s: s.dma_start(
                                    out=ddbg[h * 4 + s_:h * 4 + s_ + 1, :],
                                    in_=rsum[:]))(h, s_),
                                reads=["rsum"], writes=["dbgs"], dma=True,
                                small=True)
                    em.emit("vector", lambda v: v.reciprocal(
                        out=rsum[:], in_=pmx[0:1, 0:1024]),
                        reads=["pmx"], writes=["rsum"])
                    def mmrb(t):
                        for hh in range(2):
                            inst = t.matmul(
                                pmx[:, hh * 512:(hh + 1) * 512], ones1f,
                                rsum[0:1, hh * 512:(hh + 1) * 512],
                                start=True, stop=True)
                        return inst
                    em.emit("tensor", mmrb, reads=["rsum", "ones1f"], writes=["pmx"])
                    em.emit("scalar", lambda s: s.copy(out=rbb[:], in_=pmx[:]),
                            reads=["pmx"], writes=["rbb"])
                    em.emit("vector", lambda v: v.tensor_mul(
                        out=avn[:], in0=pav[:], in1=rbb[:]),
                        reads=["pav", "rbb"], writes=["avn"])
                    em.emit("sync",
                            (lambda h, s_: lambda s: s.dma_start(
                                out=dav[h * 128:(h + 1) * 128,
                                        s_ * 1024:(s_ + 1) * 1024],
                                in_=avn[:]))(h, s_),
                            reads=["avn"], writes=["davout"], dma=True)
        em.finalize(block, sems)
    return nc



SV_LAYOUT = {}
_off = 0
for _nm, _w in [("bo", 32), ("bbase", 32), ("bout", 32), ("bff", 32),
                ("binc", 4), ("lnlw", 4), ("lnlb", 4),
                ("ln1_w", 16), ("ln1_b", 16), ("ln2_w", 16), ("ln2_b", 16),
                ("sc_lin", 16), ("b2", 16), ("phz_re", 16), ("phz_im", 16),
                ("lc_re", 16), ("lc_im", 16), ("t0bits", 4), ("cmask", 4),
                ("brow8", 8), ("iota8", 64), ("b1c", 64)]:
    SV_LAYOUT[_nm] = (_off, _w)
    _off += _w
SV_W = _off


class Arena:
    def __init__(self, handle, kb):
        self.h = handle
        self.kb = kb

    def bf(self, off_kb, cols):
        start = off_kb * 512
        assert start + cols <= self.kb * 512
        return self.h[:][:, start:start + cols]

    def f32(self, off_kb, cols):
        full = self.h[:].bitcast(F32)
        start = off_kb * 256
        assert start + cols <= self.kb * 256
        return full[:, start:start + cols]


def build_launch3(upto="out"):
    nc = bass.Bass("TRN2", target_bir_lowering=False, debug=False)
    DI = lambda n, s, dt=F32: nc.dram_tensor(n, s, dt, kind="ExternalInput").ap()
    davq = DI("avq", [D, TQ], BF16)
    dxT = DI("xT", [D, TQ])
    dwoT = DI("woT", [D, D])
    dwbaseT = DI("wbaseT", [D, D])
    dwrT8 = DI("wrT8", [D, 8])
    dloraAT = DI("loraAT", [D, 128])
    dloraBf = DI("loraBf", [128, D])
    dwinT = DI("winT", [D, DIM])
    dw1T = DI("w1T", [4 * DIM, DFF])
    dw2T = DI("w2T", [4 * DFF, DIM])
    dwoutT = DI("woutT", [DIM, D])
    dwffT = DI("wffT", [D, D])
    dwrT4 = DI("wrT4", [D, 4])
    dAT4 = DI("AT4", [D, 32])
    dBmT4 = DI("BmT4", [128, D])
    demask8 = DI("emask8", [128, 128], BF16)
    demask4 = DI("emask4", [128, 32], BF16)
    didentb = DI("identb", [128, 128], BF16)
    donesf = DI("d_onesf", [128, 128])
    donesb = DI("d_onesb", [128, 128], BF16)
    dzerof = DI("d_zerof", [128, 1])
    dsv = {nm: DI("sv_" + nm, [128, w]) for nm, (o, w) in SV_LAYOUT.items()}

    dxout = nc.dram_tensor("xout", [D, TQ], F32, kind="ExternalOutput").ap()
    dbgkind = "ExternalOutput" if os.environ.get("DBGL3") else "Internal"
    dx1 = nc.dram_tensor("dx1", [D, TQ], F32, kind=dbgkind).ap()
    dx2 = nc.dram_tensor("dx2", [D, TQ], F32, kind=dbgkind).ap()
    dx3 = nc.dram_tensor("dx3", [D, TQ], F32, kind=dbgkind).ap()
    dhc = nc.dram_tensor("dhc", [DIM, TQ], F32, kind=dbgkind).ap()
    dyc = nc.dram_tensor("dyc", [DIM, TQ], F32, kind=dbgkind).ap()
    dw8 = nc.dram_tensor("dw8", [128, 64], F32, kind=dbgkind).ap()
    dops = nc.dram_tensor("dops", [128, 2 * NCH * 128], BF16, kind=dbgkind).ap()
    dcc_in = nc.dram_tensor("dcc_in", [128, 8], F32, kind="Internal").ap()
    dcc_out = nc.dram_tensor("dcc_out", [512, 8], F32, kind="Internal").ap()

    em = Em(strict=EMSTRICT)
    AKB = 172
    with contextlib.ExitStack() as stack:
        E = stack.enter_context
        sems = {s: E(nc.semaphore(s)) for s in ALLSEMS}
        block = E(nc.Block())
        arh = E(nc.sbuf_tensor("s_arena", [128, AKB * 512], BF16))
        ar = Arena(arh, AKB)
        sv = E(nc.sbuf_tensor("s_sv", [128, SV_W], F32))
        onesb_t = E(nc.sbuf_tensor("s_onesb", [128, 128], BF16))
        onesf_t = E(nc.sbuf_tensor("s_onesf", [128, 128], F32))
        onescol = onesb_t[:, 0:1]
        ones1f = onesf_t[0:1, :]
        identb = E(nc.sbuf_tensor("s_identb", [128, 128], BF16))
        emask8 = E(nc.sbuf_tensor("s_emask8", [128, 128], BF16))
        emask4 = E(nc.sbuf_tensor("s_emask4", [128, 32], BF16))
        wr8 = E(nc.sbuf_tensor("s_wr8", [128, NCH * 8], BF16))
        wr4 = E(nc.sbuf_tensor("s_wr4", [128, NCH * 4], BF16))
        at4 = E(nc.sbuf_tensor("s_at4", [128, NCH * 32], BF16))
        zeroc = E(nc.sbuf_tensor("s_zeroc", [128, 1], F32))
        stat = E(nc.sbuf_tensor("s_stat", [1, 3 * TQ], F32))
        p0 = E(nc.psum_tensor("p_0", [128, 1024], F32))
        p1 = E(nc.psum_tensor("p_1", [128, 1024], F32))
        p2 = E(nc.psum_tensor("p_2", [128, 1024], F32))
        p3 = E(nc.psum_tensor("p_3", [128, 1024], F32))
        pp = [p0, p1]

        def sval(nm):
            o, w = SV_LAYOUT[nm]
            return sv[:, o:o + w]

        def svc(nm, i):
            o, w = SV_LAYOUT[nm]
            return sv[:, o + i:o + i + 1]

        for nm in SV_LAYOUT:
            em.emit("sync",
                    (lambda nm: lambda s: s.dma_start(
                        out=sval(nm), in_=dsv[nm]))(nm),
                    writes=["sv"], dma=True, small=True)
        em.emit("gpsimd", lambda g: g.dma_start(out=onesb_t[:], in_=donesb),
                writes=["onescol"], dma=True)
        em.emit("gpsimd", lambda g: g.dma_start(out=onesf_t[:], in_=donesf),
                writes=["ones1f"], dma=True)
        em.emit("gpsimd", lambda g: g.dma_start(out=identb[:], in_=didentb),
                writes=["identb"], dma=True)
        em.emit("gpsimd", lambda g: g.dma_start(out=emask8[:], in_=demask8),
                writes=["emask8"], dma=True)
        em.emit("gpsimd", lambda g: g.dma_start(out=emask4[:], in_=demask4),
                writes=["emask4"], dma=True)
        em.emit("gpsimd", lambda g: g.dma_start(
            out=sb3_ap(wr8[:], NCH, 8), in_=dwrT8.rearrange("(c p) e -> p c e", p=128)),
            writes=["wr8"], dma=True)
        em.emit("gpsimd", lambda g: g.dma_start(
            out=sb3_ap(wr4[:], NCH, 4), in_=dwrT4.rearrange("(c p) e -> p c e", p=128)),
            writes=["wr4"], dma=True)
        em.emit("gpsimd", lambda g: g.dma_start(
            out=sb3_ap(at4[:], NCH, 32), in_=dAT4.rearrange("(c p) e -> p c e", p=128)),
            writes=["at4"], dma=True)
        em.emit("sync", lambda s: s.dma_start(out=zeroc[:], in_=dzerof),
                writes=["zeroc"], dma=True)

        t1v = sb3_ap(ar.bf(0, NCH * TQ), NCH, TQ)
        bB = 64
        avqv = sb3_ap(ar.bf(bB, NCH * TQ), NCH, TQ)
        wstv = sb3_ap(ar.bf(128, 2 * NCH * 128), 2 * NCH, 128)
        xt_t = [ar.f32(144 + 4 * i, TQ) for i in range(2)]
        xst = ar.f32(152, TQ)
        mu_b = ar.f32(156, TQ)
        rstd_b = ar.f32(160, TQ)
        sqsb = ar.bf(164, TQ)

        def stats_chunk(ic, nchunks, xb_ap):
            em.emit("vector",
                    (lambda xb_ap: lambda v: v.tensor_mul(
                        out=sqsb, in0=xb_ap, in1=xb_ap))(xb_ap),
                    reads=[f"tch{ic}"], writes=["sqs"])

            def mmx(xb_ap, first, last):
                def f(t):
                    for hh in range(2):
                        inst = t.matmul(p2[0:1, hh * 512:(hh + 1) * 512],
                                        onescol, xb_ap[:, hh * 512:(hh + 1) * 512],
                                        start=first, stop=last)
                    return inst
                return f
            em.emit("tensor", mmx(xb_ap, ic == 0, ic == nchunks - 1),
                    reads=[f"tch{ic}", "onescol"], writes=["p2"])

            def mmq(first, last):
                def f(t):
                    for hh in range(2):
                        inst = t.matmul(p3[0:1, hh * 512:(hh + 1) * 512],
                                        onescol, sqsb[:, hh * 512:(hh + 1) * 512],
                                        start=first, stop=last)
                    return inst
                return f
            em.emit("tensor", mmq(ic == 0, ic == nchunks - 1),
                    reads=["sqs", "onescol"], writes=["p3"])

        def stats_final(ndim):
            em.emit("vector", lambda v: v.tensor_scalar_mul(
                out=stat[0:1, 0:TQ], in0=p2[0:1, 0:TQ], scalar1=float(1.0 / ndim)),
                reads=["p2"], writes=["st_mean"])
            em.emit("vector", lambda v: v.tensor_scalar_mul(
                out=stat[0:1, TQ:2 * TQ], in0=p3[0:1, 0:TQ],
                scalar1=float(1.0 / ndim)),
                reads=["p3"], writes=["st_msq"])
            em.emit("vector", lambda v: v.tensor_mul(
                out=stat[0:1, 2 * TQ:3 * TQ], in0=stat[0:1, 0:TQ],
                in1=stat[0:1, 0:TQ]),
                reads=["st_mean"], writes=["st_m2"])
            em.emit("vector", lambda v: v.tensor_sub(
                out=stat[0:1, TQ:2 * TQ], in0=stat[0:1, TQ:2 * TQ],
                in1=stat[0:1, 2 * TQ:3 * TQ]),
                reads=["st_msq", "st_m2"], writes=["st_var"])
            em.emit("scalar", lambda s: s.activation(
                out=stat[0:1, TQ:2 * TQ], in_=stat[0:1, TQ:2 * TQ],
                func=AF.Ln, bias=EPS),
                reads=["st_var"], writes=["st_lnv"])
            em.emit("scalar", lambda s: s.activation(
                out=stat[0:1, TQ:2 * TQ], in_=stat[0:1, TQ:2 * TQ],
                func=AF.Exp, scale=np.float32(-0.5)),
                reads=["st_lnv"], writes=["st_rstd"])
            for hh in range(2):
                em.emit("tensor",
                        (lambda hh: lambda t: t.matmul(
                            p2[:, hh * 512:(hh + 1) * 512], ones1f,
                            stat[0:1, hh * 512:(hh + 1) * 512],
                            start=True, stop=True))(hh),
                        reads=["st_mean", "ones1f"], writes=["p2"])
                em.emit("tensor",
                        (lambda hh: lambda t: t.matmul(
                            p3[:, hh * 512:(hh + 1) * 512], ones1f,
                            stat[0:1, TQ + hh * 512:TQ + (hh + 1) * 512],
                            start=True, stop=True))(hh),
                        reads=["st_rstd", "ones1f"], writes=["p3"])
            em.emit("scalar", lambda s: s.copy(out=mu_b, in_=p2[:]),
                    reads=["p2"], writes=["mu_b"])
            em.emit("scalar", lambda s: s.copy(out=rstd_b, in_=p3[:]),
                    reads=["p3"], writes=["rstd_b"])

        def normalize_chunk(ic, ap_):
            em.emit("vector",
                    (lambda ap_: lambda v: v.tensor_sub(
                        out=ap_, in0=ap_, in1=mu_b))(ap_),
                    reads=[f"tch{ic}", "mu_b"], writes=[f"tch{ic}"])
            em.emit("vector",
                    (lambda ap_: lambda v: v.tensor_mul(
                        out=ap_, in0=ap_, in1=rstd_b))(ap_),
                    reads=[f"tch{ic}", "rstd_b"], writes=[f"tch{ic}"])

        def w_chunk(dsrc, oc, par):
            em.emit("gpsimd",
                    (lambda dsrc, oc, par: lambda g: g.dma_start(
                        out=wstv[:, par * NCH:(par + 1) * NCH],
                        in_=dsrc[:, oc * 128:(oc + 1) * 128].rearrange(
                            "(c p) o -> p c o", p=128)))(dsrc, oc, par),
                    writes=[f"wst{par}"], dma=True)

        def resid_tile(dsrc, rres, oc, par, extra_reads=()):
            em.emit("sync",
                    (lambda dsrc, oc, par: lambda s: s.dma_start(
                        out=xt_t[par], in_=dsrc[oc * 128:(oc + 1) * 128, :]))(
                        dsrc, oc, par),
                    reads=list(extra_reads), writes=[f"xt{par}"], dma=True)

        def big_proj_loop(dW, rhs3v, rhs_res_fn, dresid, resid_reads, bias_nm,
                          dxo, dxo_res, store_t, nic=NCH, extra_mm=None):
            for oc in range(NCH):
                par = oc % 2
                w_chunk(dW, oc, par)
                if oc == 0 and os.environ.get("L3DBG2") == "ops":
                    em.emit("sync", lambda s: s.dma_start(
                        out=dops[:, NCH * 128:2 * NCH * 128],
                        in_=wstv.rearrange("p c t -> p (c t)")[:, 0:NCH * 128]),
                        reads=["wst0"], writes=["dopsw"], dma=True)
                resid_tile(dresid, None, oc, par, extra_reads=resid_reads)

                def mmg(oc, par):
                    def f(t):
                        for hh in range(2):
                            for ic in range(nic):
                                inst = t.matmul(
                                    pp[par][:, hh * 512:(hh + 1) * 512],
                                    wstv[:, par * NCH + ic],
                                    rhs3v[:, ic, hh * 512:(hh + 1) * 512],
                                    start=(ic == 0), stop=(ic == nic - 1))
                        return inst
                    return f
                rds = [f"wst{par}"] + [rhs_res_fn(ic) for ic in range(nic)]
                if extra_mm is not None:
                    rds += ["xmm"]
                em.emit("tensor", mmg(oc, par), reads=rds, writes=[f"pp{par}"])
                dbg2 = os.environ.get("L3DBG2")
                if dbg2 == "psum":
                    em.emit("vector",
                            (lambda par: lambda v: v.tensor_copy(
                                out=xst, in_=pp[par][:]))(par),
                            reads=[f"pp{par}"], writes=["xst"])
                elif dbg2 == "xt":
                    em.emit("vector",
                            (lambda par: lambda v: v.tensor_copy(
                                out=xst, in_=xt_t[par]))(par),
                            reads=[f"xt{par}"], writes=["xst"])
                else:
                    em.emit("vector",
                            (lambda oc, par: lambda v: v.scalar_tensor_tensor(
                                out=xst, in0=pp[par][:],
                                scalar=svc(bias_nm, oc), in1=xt_t[par],
                                op0=ALU.add, op1=ALU.add))(oc, par),
                            reads=[f"pp{par}", f"xt{par}", "sv"], writes=["xst"])
                em.emit("sync",
                        (lambda oc: lambda s: s.dma_start(
                            out=dxo[oc * 128:(oc + 1) * 128, :], in_=xst))(oc),
                        reads=["xst"], writes=[dxo_res], dma=True)
                if store_t is not None:
                    em.emit("vector",
                            (lambda oc: lambda v: v.tensor_copy(
                                out=store_t[:, oc], in_=xst))(oc),
                            reads=["xst"], writes=[f"tch{oc}"])
                    stats_chunk(oc, NCH, store_t[:, oc])

        for ic in range(NCH):
            em.emit("sync",
                    (lambda ic: lambda s: s.dma_start(
                        out=avqv[:, ic], in_=davq[ic * 128:(ic + 1) * 128, :]))(ic),
                    writes=[f"avq{ic}"], dma=True)
        if os.environ.get("L3DBG2") == "ops":
            em.emit("sync", lambda s: s.dma_start(
                out=dops[:, 0:NCH * 128],
                in_=avqv.rearrange("p c t -> p (c t)")[:, 17 * 1024:17 * 1024 + NCH * 128]),
                reads=[f"avq{ic}" for ic in range(NCH)], writes=["dopsw"],
                dma=True)
        big_proj_loop(dwoT, avqv, lambda ic: f"avq{ic}", dxT, (), "bo",
                      dx1, "dx1w", t1v)

        stats_final(D)
        for ic in range(NCH):
            normalize_chunk(ic, t1v[:, ic])
        em.barrier()
        if upto == "x1":
            em.finalize(block, sems)
            return nc

        LG = ar.f32(166, 512)[:, 0:64]
        SA = ar.f32(166, 512)[:, 64:128]
        SB = ar.f32(166, 512)[:, 128:192]
        kzs = ar.f32(166, 512)[:, 192:224]
        wfm = ar.bf(168, 1024)[0:8, :]
        hsb = ar.bf(170, 1024)
        lBst = ar.bf(166, 1024)[:, 704:832]
        wr83 = sb3_ap(wr8[:], NCH, 8)

        for tc in range(8):
            def mml(tc):
                def f(t):
                    for ic in range(NCH):
                        inst = t.matmul(
                            p2[:, tc * 8:(tc + 1) * 8],
                            t1v[:, ic, tc * 128:(tc + 1) * 128],
                            wr83[:, ic],
                            start=(ic == 0), stop=(ic == NCH - 1))
                    return inst
                return f
            em.emit("tensor", mml(tc),
                    reads=["wr8"] + [f"tch{ic}" for ic in range(NCH)],
                    writes=["p2"])
            em.emit("vector",
                    (lambda tc: lambda v: v.tensor_add(
                        out=LG[:, tc * 8:(tc + 1) * 8],
                        in0=p2[:, tc * 8:(tc + 1) * 8],
                        in1=sval("brow8")))(tc),
                    reads=["p2", "sv"], writes=["LG"])
        v_lg4 = LG.rearrange("p (tc a b c) -> p tc a b c", tc=8, a=2, b=2, c=2)
        v_sa4 = SA.rearrange("p (tc a b c) -> p tc a b c", tc=8, a=2, b=2, c=2)
        v_sb4 = SB.rearrange("p (tc a b c) -> p tc a b c", tc=8, a=2, b=2, c=2)
        v_sa3 = SA.rearrange("p (tc dd c) -> p tc dd c", tc=8, dd=4, c=2)
        v_sb3 = SB.rearrange("p (tc dd c) -> p tc dd c", tc=8, dd=4, c=2)

        def cmpex(dst_lo, dst_hi, src_lo, src_hi, rsrc, rdst):
            em.emit("vector",
                    (lambda a, b, c: lambda v: v.tensor_tensor(
                        out=a, in0=b, in1=c, op=ALU.max))(dst_lo, src_lo, src_hi),
                    reads=[rsrc], writes=[rdst])
            em.emit("vector",
                    (lambda a, b, c: lambda v: v.tensor_tensor(
                        out=a, in0=b, in1=c, op=ALU.min))(dst_hi, src_lo, src_hi),
                    reads=[rsrc], writes=[rdst])

        def vcopy(dst, src, rsrc, rdst):
            em.emit("vector",
                    (lambda a, b: lambda v: v.tensor_copy(out=a, in_=b))(dst, src),
                    reads=[rsrc], writes=[rdst])

        cmpex(v_sa4[:, :, :, :, 0], v_sa4[:, :, :, :, 1],
              v_lg4[:, :, :, :, 0], v_lg4[:, :, :, :, 1], "LG", "SA")
        cmpex(v_sb4[:, :, :, 0, :], v_sb4[:, :, :, 1, :],
              v_sa4[:, :, :, 0, :], v_sa4[:, :, :, 1, :], "SA", "SB")
        cmpex(v_sa4[:, :, :, 0, 1], v_sa4[:, :, :, 1, 0],
              v_sb4[:, :, :, 0, 1], v_sb4[:, :, :, 1, 0], "SB", "SA")
        vcopy(v_sa4[:, :, :, 0, 0], v_sb4[:, :, :, 0, 0], "SB", "SA")
        vcopy(v_sa4[:, :, :, 1, 1], v_sb4[:, :, :, 1, 1], "SB", "SA")
        cmpex(v_sb4[:, :, 0, :, :], v_sb4[:, :, 1, :, :],
              v_sa4[:, :, 0, :, :], v_sa4[:, :, 1, :, :], "SA", "SB")
        cmpex(v_sa4[:, :, 0, 1, :], v_sa4[:, :, 1, 0, :],
              v_sb4[:, :, 0, 1, :], v_sb4[:, :, 1, 0, :], "SB", "SA")
        vcopy(v_sa4[:, :, 0, 0, :], v_sb4[:, :, 0, 0, :], "SB", "SA")
        vcopy(v_sa4[:, :, 1, 1, :], v_sb4[:, :, 1, 1, :], "SB", "SA")
        cmpex(v_sb3[:, :, 0:3, 1], v_sb3[:, :, 1:4, 0],
              v_sa3[:, :, 0:3, 1], v_sa3[:, :, 1:4, 0], "SA", "SB")
        vcopy(v_sb3[:, :, 0:1, 0], v_sa3[:, :, 0:1, 0], "SA", "SB")
        vcopy(v_sb3[:, :, 3:4, 1], v_sa3[:, :, 3:4, 1], "SA", "SB")
        v_sbE = SB.rearrange("p (tc e) -> p tc e", tc=8, e=8)
        v_saE = SA.rearrange("p (tc e) -> p tc e", tc=8, e=8)
        for lvl, (srcv, dstv, rs, rd) in enumerate(
                [(v_sbE, v_saE, "SB", "SA"), (v_saE, v_sbE, "SA", "SB"),
                 (v_sbE, v_saE, "SB", "SA")]):
            m = 1 << lvl
            em.emit("vector",
                    (lambda a, b, c: lambda v: v.tensor_add(
                        out=a, in0=b, in1=c))(
                        dstv[:, :, m:8], srcv[:, :, m:8], srcv[:, :, 0:8 - m]),
                    reads=[rs], writes=[rd])
            vcopy(dstv[:, :, 0:m], srcv[:, :, 0:m], rs, rd)
        sup = ar.f32(166, 512)[:, 224:288]
        em.emit("vector", lambda v: v.tensor_mul(
            out=sup, in0=SB[:], in1=sval("iota8")),
            reads=["SB", "sv"], writes=["sup"])
        em.emit("vector", lambda v: v.tensor_scalar_add(
            out=sup, in0=sup, scalar1=1.0),
            reads=["sup"], writes=["sup"])
        em.emit("vector", lambda v: v.tensor_tensor(
            out=sup, in0=sup, in1=SA[:], op=ALU.is_gt),
            reads=["sup", "SA"], writes=["sup"])
        v_sup = sup.rearrange("p (tc e) -> p tc e", tc=8, e=8)
        em.emit("vector", lambda v: v.reduce_sum(
            out=kzs[:, 0:8], in_=v_sup, axis=AX.X),
            reads=["sup"], writes=["kz"])
        em.emit("vector", lambda v: v.tensor_mul(
            out=SA[:], in0=SB[:], in1=sup),
            reads=["SB", "sup"], writes=["SA"])
        v_saE2 = SA.rearrange("p (tc e) -> p tc e", tc=8, e=8)
        em.emit("vector", lambda v: v.reduce_sum(
            out=kzs[:, 8:16], in_=v_saE2, axis=AX.X),
            reads=["SA"], writes=["tausum"])
        em.emit("vector", lambda v: v.tensor_scalar_add(
            out=kzs[:, 8:16], in0=kzs[:, 8:16], scalar1=-1.0),
            reads=["tausum"], writes=["tausum"])
        em.emit("vector", lambda v: v.reciprocal(
            out=kzs[:, 24:32], in_=kzs[:, 0:8]),
            reads=["kz"], writes=["kzr"])
        em.emit("vector", lambda v: v.tensor_mul(
            out=kzs[:, 16:24], in0=kzs[:, 8:16], in1=kzs[:, 24:32]),
            reads=["tausum", "kzr"], writes=["tau"])
        for tc in range(8):
            em.emit("vector",
                    (lambda tc: lambda v: v.tensor_scalar(
                        out=LG[:, tc * 8:(tc + 1) * 8],
                        in0=LG[:, tc * 8:(tc + 1) * 8],
                        scalar1=kzs[:, 16 + tc:17 + tc], scalar2=0.0,
                        op0=ALU.subtract, op1=ALU.max))(tc),
                    reads=["LG", "tau"], writes=["LG"])
        if os.environ.get("DBGL3"):
            em.emit("sync", lambda s: s.dma_start(out=dw8, in_=LG[:]),
                    reads=["LG"], writes=["dw8w"], dma=True)
        wbf = SA.bitcast(BF16)[:, 0:64]
        em.emit("vector", lambda v: v.tensor_copy(out=wbf, in_=LG[:]),
                reads=["LG"], writes=["SA"])
        ptr = p3[:].bitcast(BF16)
        for tc in range(8):
            em.emit("tensor",
                    (lambda tc: lambda t: t.transpose(
                        ptr[0:8, tc * 128:(tc + 1) * 128],
                        wbf[:, tc * 8:(tc + 1) * 8], identb[:]))(tc),
                    reads=["SA", "identb"], writes=["p3"])
        em.emit("vector", lambda v: v.tensor_copy(out=wfm, in_=ptr[0:8, 0:1024]),
                reads=["p3"], writes=["wfm"])

        def mmwrep(t):
            for hh in range(2):
                inst = t.matmul(p2[:, hh * 512:(hh + 1) * 512], emask8[0:8, :],
                                wfm[:, hh * 512:(hh + 1) * 512],
                                start=True, stop=True)
            return inst
        em.emit("tensor", mmwrep, reads=["wfm", "emask8"], writes=["p2"])
        em.emit("gpsimd", lambda g: g.dma_start(
            out=wstv[:, 0:NCH], in_=dloraAT.rearrange("(c p) e -> p c e", p=128)),
            writes=["wst0"], dma=True, small=True)

        def mmh(t):
            for hh in range(2):
                for ic in range(NCH):
                    inst = t.matmul(
                        p3[:, hh * 512:(hh + 1) * 512],
                        wstv[:, ic],
                        t1v[:, ic, hh * 512:(hh + 1) * 512],
                        start=(ic == 0), stop=(ic == NCH - 1))
            return inst
        em.emit("tensor", mmh,
                reads=["wst0"] + [f"tch{ic}" for ic in range(NCH)], writes=["p3"])
        em.emit("vector", lambda v: v.tensor_copy(out=hsb[:], in_=p3[:]),
                reads=["p3"], writes=["hsb"])
        em.emit("vector", lambda v: v.tensor_mul(
            out=hsb[:], in0=hsb[:], in1=p2[:]),
            reads=["hsb", "p2"], writes=["hsb"])

        def lB_load(oc):
            em.emit("gpsimd",
                    (lambda oc: lambda g: g.dma_start(
                        out=lBst, in_=dloraBf[:, oc * 128:(oc + 1) * 128]))(oc),
                    writes=["xmm"], dma=True, small=True)

        def ffn0_extra(t, oc, par, hh):
            return t.matmul(pp[par][:, hh * 512:(hh + 1) * 512], lBst,
                            hsb[:, hh * 512:(hh + 1) * 512],
                            start=False, stop=True)
        xb2v = avqv
        for oc in range(NCH):
            par = oc % 2
            w_chunk(dwbaseT, oc, par)
            lB_load(oc)
            resid_tile(dx1, None, oc, par, extra_reads=("dx1w",))

            def mmb(oc, par):
                def f(t):
                    for hh in range(2):
                        for ic in range(NCH):
                            t.matmul(
                                pp[par][:, hh * 512:(hh + 1) * 512],
                                wstv[:, par * NCH + ic],
                                t1v[:, ic, hh * 512:(hh + 1) * 512],
                                start=(ic == 0), stop=False)
                        inst = t.matmul(
                            pp[par][:, hh * 512:(hh + 1) * 512], lBst,
                            hsb[:, hh * 512:(hh + 1) * 512],
                            start=False, stop=True)
                    return inst
                return f
            em.emit("tensor", mmb(oc, par),
                    reads=[f"wst{par}", "xmm", "hsb"] +
                          [f"tch{ic}" for ic in range(NCH)],
                    writes=[f"pp{par}"])
            em.emit("vector",
                    (lambda oc, par: lambda v: v.scalar_tensor_tensor(
                        out=xst, in0=pp[par][:],
                        scalar=svc("bbase", oc), in1=xt_t[par],
                        op0=ALU.add, op1=ALU.add))(oc, par),
                    reads=[f"pp{par}", f"xt{par}", "sv"], writes=["xst"])
            em.emit("sync",
                    (lambda oc: lambda s: s.dma_start(
                        out=dx2[oc * 128:(oc + 1) * 128, :], in_=xst))(oc),
                    reads=["xst"], writes=["dx2w"], dma=True)
            em.emit("vector",
                    (lambda oc: lambda v: v.tensor_copy(
                        out=xb2v[:, oc], in_=xst))(oc),
                    reads=["xst"], writes=[f"tch{oc}"])
            stats_chunk(oc, NCH, xb2v[:, oc])
        stats_final(D)
        for ic in range(NCH):
            normalize_chunk(ic, xb2v[:, ic])
        em.barrier()
        if upto == "x2":
            em.finalize(block, sems)
            return nc

        t2v = xb2v
        h3 = sb3_ap(ar.f32(0, 4 * TQ), 4, TQ)
        hb3 = sb3_ap(ar.bf(16, 4 * TQ), 4, TQ)
        y3 = sb3_ap(ar.bf(24, 4 * TQ), 4, TQ)
        wins = sb3_ap(ar.bf(32, 2 * 4 * 128), 2 * 4, 128)
        pall = [p0, p1, p2, p3]
        for ic in range(NCH):
            par = ic % 2
            em.emit("gpsimd",
                    (lambda ic, par: lambda g: g.dma_start(
                        out=wins[:, par * 4:(par + 1) * 4],
                        in_=dwinT[ic * 128:(ic + 1) * 128, :].rearrange(
                            "p (c o) -> p c o", c=4)))(ic, par),
                    writes=[f"win{par}"], dma=True)

            def mmwin(ic, par):
                def f(t):
                    for dc in range(4):
                        for hh in range(2):
                            inst = t.matmul(
                                pall[dc][:, hh * 512:(hh + 1) * 512],
                                wins[:, par * 4 + dc],
                                t2v[:, ic, hh * 512:(hh + 1) * 512],
                                start=(ic == 0), stop=(ic == NCH - 1))
                    return inst
                return f
            em.emit("tensor", mmwin(ic, par),
                    reads=[f"win{par}", f"tch{ic}"],
                    writes=["p0w", "p1w", "p2w", "p3w"])
        for dc in range(4):
            em.emit("vector",
                    (lambda dc: lambda v: v.tensor_scalar_add(
                        out=h3[:, dc], in0=pall[dc][:],
                        scalar1=svc("binc", dc)))(dc),
                    reads=[f"p{dc}w", "sv"], writes=[f"h{dc}"])
        em.barrier()

        pch = ar.f32(128, 2048)
        C3 = sb3_ap(ar.f32(bB, 4 * TQ), 4, TQ)
        S3 = sb3_ap(ar.f32(bB + 16, 4 * TQ), 4, TQ)
        U3 = sb3_ap(ar.f32(bB + 32, 4 * TQ), 4, TQ)
        V3 = sb3_ap(ar.f32(bB + 48, 4 * TQ), 4, TQ)
        xcs = ar.f32(136, TQ)
        xss = ar.f32(140, TQ)
        scrT = ar.f32(136, 2048).rearrange("p (c o) -> p c o", c=4, o=512)
        w1s = [sb3_ap(ar.bf(144 + i, 4 * 128), 4, 128) for i in range(2)]
        w2s = sb3_ap(ar.bf(146, 16 * 128), 16, 128)
        y1v = sb3_ap(ar.bf(bB, 16 * TQ), 16, TQ)

        def pc(a, b):
            return pch[:, a:b]

        def layer_emit(l):
            lw = lambda nm: sval(nm)[:, l * 4:(l + 1) * 4]
            em.emit("vector", (lambda l: lambda v: v.tensor_mul(
                out=pc(140, 144), in0=lw("phz_re"), in1=lw("phz_re")))(l),
                reads=["sv"], writes=["pch"])
            em.emit("vector", (lambda l: lambda v: v.tensor_mul(
                out=pc(144, 148), in0=lw("phz_im"), in1=lw("phz_im")))(l),
                reads=["sv"], writes=["pch"])
            em.emit("vector", lambda v: v.tensor_add(
                out=pc(140, 144), in0=pc(140, 144), in1=pc(144, 148)),
                reads=["pch"], writes=["pch"])
            em.emit("scalar", lambda s: s.activation(
                out=pc(144, 148), in_=pc(140, 144), func=AF.Ln),
                reads=["pch"], writes=["pch"])
            em.emit("scalar", lambda s: s.activation(
                out=pc(152, 156), in_=pc(144, 148), func=AF.Exp,
                scale=np.float32(-0.5)),
                reads=["pch"], writes=["pch"])
            em.emit("vector", lambda v: v.tensor_mul(
                out=pc(144, 148), in0=pc(152, 156), in1=pc(152, 156)),
                reads=["pch"], writes=["pch"])
            em.emit("vector", lambda v: v.tensor_mul(
                out=pc(144, 148), in0=pc(144, 148), in1=pc(140, 144)),
                reads=["pch"], writes=["pch"])
            em.emit("vector", lambda v: v.tensor_scalar(
                out=pc(144, 148), in0=pc(144, 148), scalar1=-0.5, scalar2=1.5,
                op0=ALU.mult, op1=ALU.add),
                reads=["pch"], writes=["pch"])
            em.emit("vector", lambda v: v.tensor_mul(
                out=pc(152, 156), in0=pc(152, 156), in1=pc(144, 148)),
                reads=["pch"], writes=["pch"])
            em.emit("vector", lambda v: v.tensor_mul(
                out=pc(144, 148), in0=pc(140, 144), in1=pc(152, 156)),
                reads=["pch"], writes=["pch"])
            em.emit("scalar", lambda s: s.activation(
                out=pc(148, 152), in_=pc(144, 148), func=AF.Tanh),
                reads=["pch"], writes=["pch"])
            em.emit("vector", (lambda l: lambda v: v.tensor_mul(
                out=pc(156, 160), in0=lw("phz_re"), in1=pc(152, 156)))(l),
                reads=["sv", "pch"], writes=["pch"])
            em.emit("vector", (lambda l: lambda v: v.tensor_mul(
                out=pc(160, 164), in0=lw("phz_im"), in1=pc(152, 156)))(l),
                reads=["sv", "pch"], writes=["pch"])
            em.emit("vector", lambda v: v.tensor_copy(
                out=pc(0, 4), in_=pc(156, 160)), reads=["pch"], writes=["pch"])
            em.emit("vector", lambda v: v.tensor_copy(
                out=pc(48, 52), in_=pc(160, 164)), reads=["pch"], writes=["pch"])
            em.emit("vector", lambda v: v.tensor_copy(
                out=pc(96, 100), in_=pc(148, 152)), reads=["pch"], writes=["pch"])
            for k in range(1, 12):
                a, b = 4 * (k - 1), 4 * k
                em.emit("vector", (lambda a, b: lambda v: v.tensor_mul(
                    out=pc(b, b + 4), in0=pc(a, a + 4), in1=pc(a, a + 4)))(a, b),
                    reads=["pch"], writes=["pch"])
                em.emit("vector", (lambda a, b: lambda v: v.tensor_mul(
                    out=pc(140, 144), in0=pc(48 + a, 48 + a + 4),
                    in1=pc(48 + a, 48 + a + 4)))(a, b),
                    reads=["pch"], writes=["pch"])
                em.emit("vector", (lambda a, b: lambda v: v.tensor_sub(
                    out=pc(b, b + 4), in0=pc(b, b + 4), in1=pc(140, 144)))(a, b),
                    reads=["pch"], writes=["pch"])
                em.emit("vector", (lambda a, b: lambda v: v.tensor_mul(
                    out=pc(48 + b, 48 + b + 4), in0=pc(a, a + 4),
                    in1=pc(48 + a, 48 + a + 4)))(a, b),
                    reads=["pch"], writes=["pch"])
                em.emit("vector", (lambda a, b: lambda v: v.tensor_scalar_mul(
                    out=pc(48 + b, 48 + b + 4), in0=pc(48 + b, 48 + b + 4),
                    scalar1=2.0))(a, b),
                    reads=["pch"], writes=["pch"])
                if k <= 10:
                    em.emit("vector", (lambda a, b, k: lambda v: v.tensor_mul(
                        out=pc(96 + 4 * k, 100 + 4 * k), in0=pc(92 + 4 * k, 96 + 4 * k),
                        in1=pc(92 + 4 * k, 96 + 4 * k)))(a, b, k),
                        reads=["pch"], writes=["pch"])
            em.emit("vector", lambda v: v.scalar_tensor_tensor(
                out=pc(252, 256), in0=pc(40, 44), scalar=svc("t0bits", 0),
                in1=svc("t0bits", 2).broadcast_to([128, 4]),
                op0=ALU.mult, op1=ALU.add),
                reads=["pch", "sv"], writes=["pch"])
            em.emit("vector", lambda v: v.tensor_scalar_mul(
                out=pc(256, 260), in0=pc(88, 92), scalar1=svc("t0bits", 0)),
                reads=["pch", "sv"], writes=["pch"])
            em.emit("vector", lambda v: v.scalar_tensor_tensor(
                out=pc(260, 264), in0=pc(44, 48), scalar=svc("t0bits", 1),
                in1=svc("t0bits", 3).broadcast_to([128, 4]),
                op0=ALU.mult, op1=ALU.add),
                reads=["pch", "sv"], writes=["pch"])
            em.emit("vector", lambda v: v.tensor_scalar_mul(
                out=pc(264, 268), in0=pc(92, 96), scalar1=svc("t0bits", 1)),
                reads=["pch", "sv"], writes=["pch"])
            em.emit("vector", lambda v: v.tensor_mul(
                out=pc(268, 272), in0=pc(252, 256), in1=pc(260, 264)),
                reads=["pch"], writes=["pch"])
            em.emit("vector", lambda v: v.tensor_mul(
                out=pc(140, 144), in0=pc(256, 260), in1=pc(264, 268)),
                reads=["pch"], writes=["pch"])
            em.emit("vector", lambda v: v.tensor_sub(
                out=pc(268, 272), in0=pc(268, 272), in1=pc(140, 144)),
                reads=["pch"], writes=["pch"])
            em.emit("vector", lambda v: v.tensor_mul(
                out=pc(272, 276), in0=pc(252, 256), in1=pc(264, 268)),
                reads=["pch"], writes=["pch"])
            em.emit("vector", lambda v: v.tensor_mul(
                out=pc(140, 144), in0=pc(256, 260), in1=pc(260, 264)),
                reads=["pch"], writes=["pch"])
            em.emit("vector", lambda v: v.tensor_add(
                out=pc(272, 276), in0=pc(272, 276), in1=pc(140, 144)),
                reads=["pch"], writes=["pch"])
            em.emit("vector", (lambda l: lambda v: v.tensor_mul(
                out=pc(164, 168), in0=pc(156, 160), in1=lw("lc_re")))(l),
                reads=["pch", "sv"], writes=["pch"])
            em.emit("vector", (lambda l: lambda v: v.tensor_mul(
                out=pc(140, 144), in0=pc(160, 164), in1=lw("lc_im")))(l),
                reads=["pch", "sv"], writes=["pch"])
            em.emit("vector", lambda v: v.tensor_sub(
                out=pc(164, 168), in0=pc(164, 168), in1=pc(140, 144)),
                reads=["pch"], writes=["pch"])
            em.emit("vector", (lambda l: lambda v: v.tensor_mul(
                out=pc(168, 172), in0=pc(156, 160), in1=lw("lc_im")))(l),
                reads=["pch", "sv"], writes=["pch"])
            em.emit("vector", (lambda l: lambda v: v.tensor_mul(
                out=pc(140, 144), in0=pc(160, 164), in1=lw("lc_re")))(l),
                reads=["pch", "sv"], writes=["pch"])
            em.emit("vector", lambda v: v.tensor_add(
                out=pc(168, 172), in0=pc(168, 172), in1=pc(140, 144)),
                reads=["pch"], writes=["pch"])
            em.emit("vector", lambda v: v.tensor_scalar_mul(
                out=pc(168, 172), in0=pc(168, 172), scalar1=-1.0),
                reads=["pch"], writes=["pch"])
            em.emit("vector", lambda v: v.tensor_copy(
                out=C3[:, :, 0:1], in_=pc(268, 272).rearrange("p (c o) -> p c o", o=1)),
                reads=["pch"], writes=["B0"])
            em.emit("vector", lambda v: v.tensor_copy(
                out=S3[:, :, 0:1], in_=pc(272, 276).rearrange("p (c o) -> p c o", o=1)),
                reads=["pch"], writes=["B1"])
            for k in range(10):
                w = 1 << k
                ere = pc(4 * k, 4 * k + 4).rearrange("p (c o) -> p c o", o=1)
                eim = pc(48 + 4 * k, 52 + 4 * k).rearrange("p (c o) -> p c o", o=1)
                erb = ere.broadcast_to([128, 4, w])
                eib = eim.broadcast_to([128, 4, w])
                em.emit("vector", (lambda k, w, erb: lambda v: v.tensor_mul(
                    out=C3[:, :, w:2 * w], in0=C3[:, :, 0:w], in1=erb))(k, w, erb),
                    reads=["B0", "pch"], writes=["B0"])
                em.emit("vector", (lambda k, w, eib: lambda v: v.tensor_mul(
                    out=scrT[:, :, 0:w],
                    in0=S3[:, :, 0:w], in1=eib))(k, w, eib),
                    reads=["B1", "pch"], writes=["xcs"])
                em.emit("vector", (lambda k, w: lambda v: v.tensor_sub(
                    out=C3[:, :, w:2 * w], in0=C3[:, :, w:2 * w],
                    in1=scrT[:, :, 0:w]))(k, w),
                    reads=["B0", "xcs"], writes=["B0"])
                em.emit("vector", (lambda k, w, eib: lambda v: v.tensor_mul(
                    out=S3[:, :, w:2 * w], in0=C3[:, :, 0:w], in1=eib))(k, w, eib),
                    reads=["B0", "B1", "pch"], writes=["B1"])
                em.emit("vector", (lambda k, w, erb: lambda v: v.tensor_mul(
                    out=scrT[:, :, 0:w],
                    in0=S3[:, :, 0:w], in1=erb))(k, w, erb),
                    reads=["B1", "pch"], writes=["xcs"])
                em.emit("vector", (lambda k, w: lambda v: v.tensor_add(
                    out=S3[:, :, w:2 * w], in0=S3[:, :, w:2 * w],
                    in1=scrT[:, :, 0:w]))(k, w),
                    reads=["B1", "xcs"], writes=["B1"])
            for dc in range(4):
                em.emit("vector", (lambda dc: lambda v: v.tensor_copy(
                    out=hb3[:, dc], in_=h3[:, dc]))(dc),
                    reads=[f"h{dc}"], writes=[f"tch{dc}"])
                stats_chunk(dc, 4, hb3[:, dc])
            stats_final(DIM)
            for dc in range(4):
                em.emit("vector", (lambda dc: lambda v: v.tensor_sub(
                    out=y3[:, dc], in0=hb3[:, dc], in1=mu_b))(dc),
                    reads=[f"tch{dc}", "mu_b"], writes=[f"y{dc}"])
                em.emit("vector", (lambda dc: lambda v: v.tensor_mul(
                    out=y3[:, dc], in0=y3[:, dc], in1=rstd_b))(dc),
                    reads=[f"y{dc}", "rstd_b"], writes=[f"y{dc}"])
                em.emit("vector", (lambda dc, l: lambda v: v.tensor_scalar(
                    out=y3[:, dc], in0=y3[:, dc],
                    scalar1=svc("ln1_w", l * 4 + dc), scalar2=svc("ln1_b", l * 4 + dc),
                    op0=ALU.mult, op1=ALU.add))(dc, l),
                    reads=[f"y{dc}", "sv"], writes=[f"y{dc}"])
                em.emit("scalar", (lambda dc: lambda s: s.activation(
                    out=y3[:, dc], in_=y3[:, dc], func=AF.Silu))(dc),
                    reads=[f"y{dc}"], writes=[f"y{dc}"])
            for dc in range(4):
                pnb = pc(148, 152)[:, dc:dc + 1].broadcast_to([128, TQ])
                em.emit("vector", (lambda dc: lambda v: v.tensor_mul(
                    out=xcs, in0=y3[:, dc], in1=C3[:, dc]))(dc),
                    reads=[f"y{dc}", "B0"], writes=["xcs"])
                em.emit("vector", (lambda dc, pnb: lambda v: v.tensor_tensor_scan(
                    out=U3[:, dc], data0=pnb, data1=xcs, initial=0.0,
                    op0=ALU.mult, op1=ALU.add))(dc, pnb),
                    reads=["xcs", "pch"], writes=["B2"])
                em.emit("vector", (lambda dc: lambda v: v.tensor_mul(
                    out=xss, in0=y3[:, dc], in1=S3[:, dc]))(dc),
                    reads=[f"y{dc}", "B1"], writes=["xss"])
                em.emit("vector", (lambda dc, pnb: lambda v: v.tensor_tensor_scan(
                    out=V3[:, dc], data0=pnb, data1=xss, initial=0.0,
                    op0=ALU.mult, op1=ALU.add))(dc, pnb),
                    reads=["xss", "pch"], writes=["B3"])
            for dc in range(4):
                em.emit("vector", (lambda dc: lambda v: v.tensor_copy(
                    out=pc(212, 220)[:, dc:dc + 1], in_=U3[:, dc, TQ - 1:TQ]))(dc),
                    reads=["B2"], writes=["pch"])
                em.emit("vector", (lambda dc: lambda v: v.tensor_copy(
                    out=pc(212, 220)[:, 4 + dc:5 + dc], in_=V3[:, dc, TQ - 1:TQ]))(dc),
                    reads=["B3"], writes=["pch"])
            em.emit("sync", lambda s: s.dma_start(out=dcc_in, in_=pc(212, 220)),
                    reads=["pch"], writes=["dccin"], dma=True)
            em.emit("gpsimd", lambda g: g.collective_compute(
                "AllGather", ALU.bypass,
                replica_groups=[[0, 1, 2, 3], [4, 5, 6, 7]],
                ins=[dcc_in], outs=[dcc_out]),
                reads=["dccin"], writes=["dccout"], coll=True)
            for r in range(4):
                em.emit("sync",
                        (lambda r: lambda s: s.dma_start(
                            out=pc(220, 252)[:, r * 8:(r + 1) * 8],
                            in_=dcc_out[r * 128:(r + 1) * 128, :]))(r),
                        reads=["dccout"], writes=["allg"], dma=True)
            em.emit("vector", lambda v: v.tensor_copy(
                out=pc(136, 140), in_=pc(96 + 40, 96 + 44)),
                reads=["pch"], writes=["pch"])
            prevu, prevv = (164, 168), (168, 172)
            for gi in range(3):
                du = (172 + 4 * gi, 176 + 4 * gi)
                dv_ = (188 + 4 * gi, 192 + 4 * gi)
                em.emit("vector", (lambda prevu, du: lambda v: v.tensor_mul(
                    out=pc(*du), in0=pc(*prevu), in1=pc(136, 140)))(prevu, du),
                    reads=["pch"], writes=["pch"])
                em.emit("vector", (lambda gi, du: lambda v: v.tensor_add(
                    out=pc(*du), in0=pc(*du),
                    in1=pc(220, 252)[:, gi * 8:gi * 8 + 4]))(gi, du),
                    reads=["pch", "allg"], writes=["pch"])
                em.emit("vector", (lambda prevv, dv_: lambda v: v.tensor_mul(
                    out=pc(*dv_), in0=pc(*prevv), in1=pc(136, 140)))(prevv, dv_),
                    reads=["pch"], writes=["pch"])
                em.emit("vector", (lambda gi, dv_: lambda v: v.tensor_add(
                    out=pc(*dv_), in0=pc(*dv_),
                    in1=pc(220, 252)[:, gi * 8 + 4:gi * 8 + 8]))(gi, dv_),
                    reads=["pch", "allg"], writes=["pch"])
                prevu, prevv = du, dv_
            em.emit("vector", lambda v: v.tensor_scalar_mul(
                out=pc(204, 208), in0=pc(164, 168), scalar1=svc("cmask", 0)),
                reads=["pch", "sv"], writes=["pch"])
            em.emit("vector", lambda v: v.tensor_scalar_mul(
                out=pc(208, 212), in0=pc(168, 172), scalar1=svc("cmask", 0)),
                reads=["pch", "sv"], writes=["pch"])
            for gi in range(3):
                em.emit("vector", (lambda gi: lambda v: v.scalar_tensor_tensor(
                    out=pc(204, 208), in0=pc(172 + 4 * gi, 176 + 4 * gi),
                    scalar=svc("cmask", gi + 1), in1=pc(204, 208),
                    op0=ALU.mult, op1=ALU.add))(gi),
                    reads=["pch", "sv"], writes=["pch"])
                em.emit("vector", (lambda gi: lambda v: v.scalar_tensor_tensor(
                    out=pc(208, 212), in0=pc(188 + 4 * gi, 192 + 4 * gi),
                    scalar=svc("cmask", gi + 1), in1=pc(208, 212),
                    op0=ALU.mult, op1=ALU.add))(gi),
                    reads=["pch", "sv"], writes=["pch"])
            for dc in range(4):
                pnb = pc(148, 152)[:, dc:dc + 1].broadcast_to([128, TQ])
                zb = zeroc[:].broadcast_to([128, TQ])
                em.emit("vector", (lambda dc, pnb, zb: lambda v: v.tensor_tensor_scan(
                    out=xcs, data0=pnb, data1=zb, initial=pc(204, 208)[:, dc:dc + 1],
                    op0=ALU.mult, op1=ALU.add))(dc, pnb, zb),
                    reads=["pch", "zeroc"], writes=["xcs"])
                em.emit("vector", (lambda dc: lambda v: v.tensor_add(
                    out=U3[:, dc], in0=U3[:, dc], in1=xcs))(dc),
                    reads=["B2", "xcs"], writes=["B2"])
                em.emit("vector", (lambda dc, pnb, zb: lambda v: v.tensor_tensor_scan(
                    out=xss, data0=pnb, data1=zb, initial=pc(208, 212)[:, dc:dc + 1],
                    op0=ALU.mult, op1=ALU.add))(dc, pnb, zb),
                    reads=["pch", "zeroc"], writes=["xss"])
                em.emit("vector", (lambda dc: lambda v: v.tensor_add(
                    out=V3[:, dc], in0=V3[:, dc], in1=xss))(dc),
                    reads=["B3", "xss"], writes=["B3"])
            for dc in range(4):
                em.emit("vector", (lambda dc: lambda v: v.tensor_mul(
                    out=xcs, in0=C3[:, dc], in1=U3[:, dc]))(dc),
                    reads=["B0", "B2"], writes=["xcs"])
                em.emit("vector", (lambda dc: lambda v: v.tensor_mul(
                    out=xss, in0=S3[:, dc], in1=V3[:, dc]))(dc),
                    reads=["B1", "B3"], writes=["xss"])
                em.emit("vector", lambda v: v.tensor_add(
                    out=xcs, in0=xcs, in1=xss),
                    reads=["xcs", "xss"], writes=["xcs"])
                if l == 0 and os.environ.get("DBGL3"):
                    em.emit("sync",
                            (lambda dc: lambda s: s.dma_start(
                                out=dyc[dc * 128:(dc + 1) * 128, :], in_=xcs))(dc),
                            reads=["xcs"], writes=["dycw"], dma=True)
                em.emit("vector", (lambda dc, l: lambda v: v.scalar_tensor_tensor(
                    out=h3[:, dc], in0=xcs, scalar=svc("sc_lin", l * 4 + dc),
                    in1=h3[:, dc], op0=ALU.mult, op1=ALU.add))(dc, l),
                    reads=["xcs", "sv", f"h{dc}"], writes=[f"h{dc}"])
            em.barrier()
            for dc in range(4):
                em.emit("vector", (lambda dc: lambda v: v.tensor_copy(
                    out=hb3[:, dc], in_=h3[:, dc]))(dc),
                    reads=[f"h{dc}"], writes=[f"tch{dc}"])
                stats_chunk(dc, 4, hb3[:, dc])
            stats_final(DIM)
            for dc in range(4):
                em.emit("vector", (lambda dc: lambda v: v.tensor_sub(
                    out=y3[:, dc], in0=hb3[:, dc], in1=mu_b))(dc),
                    reads=[f"tch{dc}", "mu_b"], writes=[f"y{dc}"])
                em.emit("vector", (lambda dc: lambda v: v.tensor_mul(
                    out=y3[:, dc], in0=y3[:, dc], in1=rstd_b))(dc),
                    reads=[f"y{dc}", "rstd_b"], writes=[f"y{dc}"])
                em.emit("vector", (lambda dc, l: lambda v: v.tensor_scalar(
                    out=y3[:, dc], in0=y3[:, dc],
                    scalar1=svc("ln2_w", l * 4 + dc), scalar2=svc("ln2_b", l * 4 + dc),
                    op0=ALU.mult, op1=ALU.add))(dc, l),
                    reads=[f"y{dc}", "sv"], writes=[f"y{dc}"])
            for fc in range(16):
                par = fc % 2
                em.emit("gpsimd",
                        (lambda l, fc, par: lambda g: g.dma_start(
                            out=w1s[par][:],
                            in_=dw1T[l * 512:(l + 1) * 512,
                                     fc * 128:(fc + 1) * 128].rearrange(
                                "(c p) f -> p c f", p=128)))(l, fc, par),
                        writes=[f"w1s{par}"], dma=True)

                def mm1(fc, par):
                    def f(t):
                        for hh in range(2):
                            for dc in range(4):
                                inst = t.matmul(
                                    pp[par][:, hh * 512:(hh + 1) * 512],
                                    w1s[par][:, dc],
                                    y3[:, dc, hh * 512:(hh + 1) * 512],
                                    start=(dc == 0), stop=(dc == 3))
                        return inst
                    return f
                em.emit("tensor", mm1(fc, par),
                        reads=[f"w1s{par}"] + [f"y{dc}" for dc in range(4)],
                        writes=[f"pp{par}"])
                em.emit("scalar",
                        (lambda l, fc, par: lambda s: s.activation(
                            out=y1v[:, fc], in_=pp[par][:], func=AF.Silu,
                            bias=svc("b1c", l * 16 + fc)))(l, fc, par),
                        reads=[f"pp{par}", "sv"], writes=[f"B{fc // 8}"])
            for dc in range(4):
                em.emit("gpsimd",
                        (lambda l, dc: lambda g: g.dma_start(
                            out=w2s[:],
                            in_=dw2T[l * 2048:(l + 1) * 2048,
                                     dc * 128:(dc + 1) * 128].rearrange(
                                "(c p) f -> p c f", p=128)))(l, dc),
                        writes=["w2s"], dma=True)

                def mm2(dc):
                    def f(t):
                        for hh in range(2):
                            for fc in range(16):
                                inst = t.matmul(
                                    p2[:, hh * 512:(hh + 1) * 512],
                                    w2s[:, fc],
                                    y1v[:, fc, hh * 512:(hh + 1) * 512],
                                    start=(fc == 0), stop=(fc == 15))
                        return inst
                    return f
                em.emit("tensor", mm2(dc),
                        reads=["w2s", "B0", "B1"], writes=["p2"])
                em.emit("vector",
                        (lambda dc, l: lambda v: v.scalar_tensor_tensor(
                            out=h3[:, dc], in0=p2[:], scalar=svc("b2", l * 4 + dc),
                            in1=h3[:, dc], op0=ALU.add, op1=ALU.add))(dc, l),
                        reads=["p2", "sv", f"h{dc}"], writes=[f"h{dc}"])
            em.barrier()

        for l in range(DEPTH4):
            layer_emit(l)

        if os.environ.get("DBGL3"):
            for dc in range(4):
                em.emit("sync",
                        (lambda dc: lambda s: s.dma_start(
                            out=dhc[dc * 128:(dc + 1) * 128, :], in_=h3[:, dc]))(dc),
                        reads=[f"h{dc}"], writes=["dhcw"], dma=True)
        if upto == "hc":
            em.finalize(block, sems)
            return nc

        for dc in range(4):
            em.emit("vector", (lambda dc: lambda v: v.tensor_copy(
                out=hb3[:, dc], in_=h3[:, dc]))(dc),
                reads=[f"h{dc}"], writes=[f"tch{dc}"])
            stats_chunk(dc, 4, hb3[:, dc])
        stats_final(DIM)
        for dc in range(4):
            em.emit("vector", (lambda dc: lambda v: v.tensor_sub(
                out=y3[:, dc], in0=hb3[:, dc], in1=mu_b))(dc),
                reads=[f"tch{dc}", "mu_b"], writes=[f"y{dc}"])
            em.emit("vector", (lambda dc: lambda v: v.tensor_mul(
                out=y3[:, dc], in0=y3[:, dc], in1=rstd_b))(dc),
                reads=[f"y{dc}", "rstd_b"], writes=[f"y{dc}"])
            em.emit("vector", (lambda dc: lambda v: v.tensor_scalar(
                out=y3[:, dc], in0=y3[:, dc],
                scalar1=svc("lnlw", dc), scalar2=svc("lnlb", dc),
                op0=ALU.mult, op1=ALU.add))(dc),
                reads=[f"y{dc}", "sv"], writes=[f"y{dc}"])
        em.barrier()

        t3v = sb3_ap(ar.bf(bB, NCH * TQ), NCH, TQ)
        wouts = [sb3_ap(ar.bf(32 + i, 4 * 128), 4, 128) for i in range(2)]
        for oc in range(NCH):
            par = oc % 2
            em.emit("gpsimd",
                    (lambda oc, par: lambda g: g.dma_start(
                        out=wouts[par][:],
                        in_=dwoutT[:, oc * 128:(oc + 1) * 128].rearrange(
                            "(c p) o -> p c o", p=128)))(oc, par),
                    writes=[f"wo{par}"], dma=True)
            resid_tile(dx2, None, oc, par, extra_reads=("dx2w",))

            def mmout(oc, par):
                def f(t):
                    for hh in range(2):
                        for dc in range(4):
                            inst = t.matmul(
                                pp[par][:, hh * 512:(hh + 1) * 512],
                                wouts[par][:, dc],
                                y3[:, dc, hh * 512:(hh + 1) * 512],
                                start=(dc == 0), stop=(dc == 3))
                    return inst
                return f
            em.emit("tensor", mmout(oc, par),
                    reads=[f"wo{par}"] + [f"y{dc}" for dc in range(4)],
                    writes=[f"pp{par}"])
            em.emit("vector",
                    (lambda oc, par: lambda v: v.scalar_tensor_tensor(
                        out=xst, in0=pp[par][:], scalar=svc("bout", oc),
                        in1=xt_t[par], op0=ALU.add, op1=ALU.add))(oc, par),
                    reads=[f"pp{par}", f"xt{par}", "sv"], writes=["xst"])
            em.emit("sync",
                    (lambda oc: lambda s: s.dma_start(
                        out=dx3[oc * 128:(oc + 1) * 128, :], in_=xst))(oc),
                    reads=["xst"], writes=["dx3w"], dma=True)
            em.emit("vector",
                    (lambda oc: lambda v: v.tensor_copy(
                        out=t3v[:, oc], in_=xst))(oc),
                    reads=["xst"], writes=[f"tch{oc}"])
            stats_chunk(oc, NCH, t3v[:, oc])
        stats_final(D)
        for ic in range(NCH):
            normalize_chunk(ic, t3v[:, ic])
        em.barrier()
        if upto == "x3":
            em.finalize(block, sems)
            return nc

        LG4 = ar.f32(166, 512)[:, 0:32]
        SA4 = ar.f32(166, 512)[:, 32:64]
        kz4 = ar.f32(166, 512)[:, 64:96]
        gfm = ar.bf(168, 1024)[0:4, :]
        h4b = ar.bf(170, 1024)[0:32, :]
        bmst = ar.bf(166, 1024)[:, 704:832]
        wr43 = sb3_ap(wr4[:], NCH, 4)
        at43 = sb3_ap(at4[:], NCH, 32)
        for tc in range(8):
            def mml4(tc):
                def f(t):
                    for ic in range(NCH):
                        inst = t.matmul(
                            p2[:, tc * 4:(tc + 1) * 4],
                            t3v[:, ic, tc * 128:(tc + 1) * 128],
                            wr43[:, ic],
                            start=(ic == 0), stop=(ic == NCH - 1))
                    return inst
                return f
            em.emit("tensor", mml4(tc),
                    reads=["wr4"] + [f"tch{ic}" for ic in range(NCH)],
                    writes=["p2"])
            em.emit("vector",
                    (lambda tc: lambda v: v.tensor_copy(
                        out=LG4[:, tc * 4:(tc + 1) * 4],
                        in_=p2[:, tc * 4:(tc + 1) * 4]))(tc),
                    reads=["p2"], writes=["LG4"])
        v_lg4e = LG4.rearrange("p (tc e) -> p tc e", tc=8, e=4)
        em.emit("vector", lambda v: v.reduce_max(
            out=kz4[:, 0:8], in_=v_lg4e, axis=AX.X),
            reads=["LG4"], writes=["rm4"])
        for tc in range(8):
            em.emit("vector",
                    (lambda tc: lambda v: v.tensor_scalar_sub(
                        out=SA4[:, tc * 4:(tc + 1) * 4],
                        in0=LG4[:, tc * 4:(tc + 1) * 4],
                        scalar1=kz4[:, tc:tc + 1]))(tc),
                    reads=["LG4", "rm4"], writes=["SA4"])
        em.emit("scalar", lambda s: s.activation(
            out=SA4[:], in_=SA4[:], func=AF.Exp),
            reads=["SA4"], writes=["SA4"])
        v_sa4e = SA4.rearrange("p (tc e) -> p tc e", tc=8, e=4)
        em.emit("vector", lambda v: v.reduce_sum(
            out=kz4[:, 8:16], in_=v_sa4e, axis=AX.X),
            reads=["SA4"], writes=["sm4"])
        em.emit("vector", lambda v: v.reciprocal(
            out=kz4[:, 16:24], in_=kz4[:, 8:16]),
            reads=["sm4"], writes=["rc4"])
        em.emit("vector", lambda v: v.tensor_scalar_mul(
            out=kz4[:, 16:24], in0=kz4[:, 16:24], scalar1=2.0),
            reads=["rc4"], writes=["rc4"])
        for tc in range(8):
            em.emit("vector",
                    (lambda tc: lambda v: v.tensor_scalar_mul(
                        out=SA4[:, tc * 4:(tc + 1) * 4],
                        in0=SA4[:, tc * 4:(tc + 1) * 4],
                        scalar1=kz4[:, 16 + tc:17 + tc]))(tc),
                    reads=["SA4", "rc4"], writes=["SA4"])
        gbf = LG4.bitcast(BF16)[:, 0:32]
        em.emit("vector", lambda v: v.tensor_copy(out=gbf, in_=SA4[:]),
                reads=["SA4"], writes=["LG4"])
        ptr4 = p3[:].bitcast(BF16)
        for tc in range(8):
            em.emit("tensor",
                    (lambda tc: lambda t: t.transpose(
                        ptr4[0:4, tc * 128:(tc + 1) * 128],
                        gbf[:, tc * 4:(tc + 1) * 4], identb[:]))(tc),
                    reads=["LG4", "identb"], writes=["p3"])
        em.emit("vector", lambda v: v.tensor_copy(out=gfm, in_=ptr4[0:4, 0:1024]),
                reads=["p3"], writes=["gfm"])

        def mmwrep4(t):
            for hh in range(2):
                inst = t.matmul(p2[0:32, hh * 512:(hh + 1) * 512], emask4[0:4, :],
                                gfm[:, hh * 512:(hh + 1) * 512],
                                start=True, stop=True)
            return inst
        em.emit("tensor", mmwrep4, reads=["gfm", "emask4"], writes=["p2"])

        def mmh4(t):
            for hh in range(2):
                for ic in range(NCH):
                    inst = t.matmul(
                        p3[0:32, hh * 512:(hh + 1) * 512],
                        at43[:, ic],
                        t3v[:, ic, hh * 512:(hh + 1) * 512],
                        start=(ic == 0), stop=(ic == NCH - 1))
            return inst
        em.emit("tensor", mmh4,
                reads=["at4"] + [f"tch{ic}" for ic in range(NCH)], writes=["p3"])
        em.emit("vector", lambda v: v.tensor_copy(out=h4b[:], in_=p3[0:32, :]),
                reads=["p3"], writes=["h4b"])
        em.emit("vector", lambda v: v.tensor_mul(
            out=h4b[:], in0=h4b[:], in1=p2[0:32, :]),
            reads=["h4b", "p2"], writes=["h4b"])

        for oc in range(NCH):
            par = oc % 2
            w_chunk(dwffT, oc, par)
            em.emit("gpsimd",
                    (lambda oc: lambda g: g.dma_start(
                        out=bmst, in_=dBmT4[:, oc * 128:(oc + 1) * 128]))(oc),
                    writes=["xmm"], dma=True, small=True)
            resid_tile(dx3, None, oc, par, extra_reads=("dx3w",))

            def mmf(oc, par):
                def f(t):
                    for hh in range(2):
                        for ic in range(NCH):
                            t.matmul(
                                pp[par][:, hh * 512:(hh + 1) * 512],
                                wstv[:, par * NCH + ic],
                                t3v[:, ic, hh * 512:(hh + 1) * 512],
                                start=(ic == 0), stop=False)
                        inst = t.matmul(
                            pp[par][:, hh * 512:(hh + 1) * 512], bmst[0:32, :],
                            h4b[:, hh * 512:(hh + 1) * 512],
                            start=False, stop=True)
                    return inst
                return f
            em.emit("tensor", mmf(oc, par),
                    reads=[f"wst{par}", "xmm", "h4b"] +
                          [f"tch{ic}" for ic in range(NCH)],
                    writes=[f"pp{par}"])
            em.emit("vector",
                    (lambda oc, par: lambda v: v.scalar_tensor_tensor(
                        out=xst, in0=pp[par][:], scalar=svc("bff", oc),
                        in1=xt_t[par], op0=ALU.add, op1=ALU.add))(oc, par),
                    reads=[f"pp{par}", f"xt{par}", "sv"], writes=["xst"])
            em.emit("sync",
                    (lambda oc: lambda s: s.dma_start(
                        out=dxout[oc * 128:(oc + 1) * 128, :], in_=xst))(oc),
                    reads=["xst"], writes=["dxoutw"], dma=True)
        em.finalize(block, sems)
        return nc


def sb3_ap(ap, nch, w):
    return ap.rearrange("p (c t) -> p c t", c=nch, t=w)


def kernel(**inputs):
    cores = list(range(8))
    staged = [stage_core(inputs, c) for c in cores]
    nc1 = build_launch1()
    in1 = [{"xT": staged[c]["xT"], "d_onesf": staged[c]["onesf128"],
            "d_onesb": staged[c]["onesb128"]} for c in cores]
    res1 = run_bass_kernel_spmd(nc1, in1, core_ids=cores)
    xlnq = [res1.results[c]["xlnq"] for c in cores]
    if os.environ.get("KCHECK"):
        for c in cores:
            nf = np.isnan(np.asarray(xlnq[c], np.float32)).mean()
            if nf > 0:
                print(f"KCHECK: launch1 core {c} nan frac {nf:.3f}")

    xlnf = [np.concatenate([xlnq[b * 4 + s] for s in range(4)], axis=1)
            for b in range(B)]

    nc2 = build_launch2()
    in2 = []
    for c in cores:
        b = c // 4
        st = staged[c]
        in2.append({"xlnf": xlnf[b], "wqT": st["wqT"], "wkT": st["wkT"],
                    "wvT": st["wvT"], "bq": st["bq"], "bk": st["bk"],
                    "bvxf": st["bvxf"], "masks": st["masks"],
                    "d_onesf": st["onesf128"], "d_onesb": st["onesb128"]})
    res2 = run_bass_kernel_spmd(nc2, in2, core_ids=cores)
    av = [res2.results[c]["av"] for c in cores]
    if os.environ.get("KCHECK"):
        for c in cores:
            nf = np.isnan(np.asarray(av[c], np.float32)).mean()
            if nf > 0:
                print(f"KCHECK: launch2 core {c} nan frac {nf:.3f}")
    if os.environ.get("DBGL2"):
        return xlnq, av, res2

    return _kernel_l3(inputs, staged, cores, xlnq, av)


def _kernel_l3(inputs, staged, cores, xlnq, av):
    avf = [np.concatenate([av[b * 4 + s] for s in range(4)], axis=0)
           for b in range(B)]

    upto = os.environ.get("L3UPTO", "out")
    nc3 = build_launch3(upto)
    svmap = {"bo": "bo", "bbase": "bbase", "bout": "bout", "bff": "bff",
             "binc": "binc", "lnlw": "lnlw", "lnlb": "lnlb",
             "ln1_w": "ln1_w", "ln1_b": "ln1_b", "ln2_w": "ln2_w",
             "ln2_b": "ln2_b", "sc_lin": "sc_lin", "b2": "b2",
             "phz_re": "phz_re", "phz_im": "phz_im", "lc_re": "lc_re",
             "lc_im": "lc_im", "t0bits": "t0bits", "cmask": "cmask",
             "brow8": "brow8", "iota8": "iota8", "b1c": "b1c"}
    in3 = []
    for c in cores:
        b, sub = c // 4, c % 4
        st = staged[c]
        m = {"avq": np.ascontiguousarray(avf[b][:, sub * TQ:(sub + 1) * TQ]),
             "xT": st["xT"], "woT": st["woT"], "wbaseT": st["wbaseT"],
             "wrT8": st["wrT8"], "loraAT": st["loraAT"], "loraBf": st["loraBf"],
             "winT": st["winT"], "w1T": st["w1T"], "w2T": st["w2T"],
             "woutT": st["woutT"], "wffT": st["wffT"], "wrT4": st["wrT4"],
             "AT4": st["AT4"], "BmT4": st["BmT4"], "emask8": st["emask8"],
             "emask4": st["emask4"], "identb": st["identb"],
             "d_onesf": st["onesf128"], "d_onesb": st["onesb128"],
             "d_zerof": st["zerof"]}
        for k, sk in svmap.items():
            m["sv_" + k] = st[sk]
        in3.append(m)
    res3 = run_bass_kernel_spmd(nc3, in3, core_ids=cores)
    if os.environ.get("DBGL3"):
        return res3
    out = np.empty((B, T, D), np.float32)
    for c in cores:
        b, sub = c // 4, c % 4
        out[b, sub * TQ:(sub + 1) * TQ, :] = res3.results[c]["xout"].T
    return out
